# version 03 (final)
# baseline (speedup 1.0000x reference)
import os
import numpy as np
import ml_dtypes
from contextlib import ExitStack

import concourse.bass as bass
import concourse.mybir as mybir
from concourse.bass_utils import run_bass_kernel_spmd

F32 = mybir.dt.float32
BF16 = mybir.dt.bfloat16
AF = mybir.ActivationFunctionType
ALU = mybir.AluOpType
AX = mybir.AxisListType

B, S, D = 8, 2048, 1024
NCHUNK = D // 128
NTILE = S // 128
TB = 512
NTB = S // TB

ARENA_KB = 207
PAGE = 512
SAME_ENGINE_SYNC = os.environ.get("MK_SES", "1") == "1"
STRICT = os.environ.get("MK_STRICT", "0") == "1"

ENG = ["tensor", "vector", "scalar", "gpsimd", "sync"]
DMAQ = ["gpsimd", "sync", "scalar"]
NSUB = 12
ALLE = ENG + [f"dq_{q}_{r}" for q in DMAQ for r in range(NSUB)]
EIDX = {e: i for i, e in enumerate(ALLE)}
NE = len(ALLE)


class Op:
    __slots__ = ("engine", "fn", "seq", "waits", "inc", "is_dma", "queue", "vc", "name")

    def __init__(self, engine, fn, is_dma, queue, name):
        self.engine, self.fn, self.is_dma, self.queue, self.name = \
            engine, fn, is_dma, queue, name
        self.waits = []
        self.inc = False


class Buf:
    __slots__ = ("k", "name", "dtype", "p0", "npart", "byte0", "free", "esz")

    def __init__(self, k, name, dtype, p0, npart, byte0, free):
        self.k, self.name, self.dtype = k, name, dtype
        self.p0, self.npart, self.byte0, self.free = p0, npart, byte0, free
        self.esz = 4 if dtype in (F32, mybir.dt.int32) else 2

    def ap(self):
        if self.dtype == F32:
            h = self.k.arena_f32
        elif self.dtype == BF16:
            h = self.k.arena_bf16
        else:
            h = self.k.arena_i32
        e0 = self.byte0 // self.esz
        return h.ap()[self.p0:self.p0 + self.npart, e0:e0 + self.free]

    def __getitem__(self, key):
        return self.ap()[key]


class Kern:
    def __init__(self, nc):
        self.nc = nc
        self.q_ops = {e: [] for e in ALLE}
        self.streams = {e: [] for e in ENG}
        self.arena_bytes = ARENA_KB * 1024
        self.arena = nc.alloc_sbuf_tensor("arena", [128, self.arena_bytes], mybir.dt.uint8)
        self.arena_f32 = self.arena.bitcast(F32)
        self.arena_bf16 = self.arena.bitcast(BF16)
        self.arena_i32 = self.arena.bitcast(mybir.dt.int32)
        self.cursor = 0
        self.hwm = 0
        self.banks = [nc.alloc_psum_tensor(f"bank{i}", [128, 512], F32) for i in range(8)]
        self.banks_bf = [b.bitcast(BF16) for b in self.banks]
        self.bank_rr = 0
        self.npages = self.arena_bytes // PAGE
        ncell = 4 * self.npages
        self.w_cell = [None] * ncell
        self.r_cell = [None] * ncell
        self.dram_w = {}
        self.dram_r = {}
        self.psum_w = [None] * 8
        self.psum_r = [None] * 8
        self.psum_lastr = [None] * 8
        self.vc_now = {e: [0] * NE for e in ENG}
        self.const_slots = {}
        self.constpool = None
        self.dma_rr = {q: 0 for q in DMAQ}

    def alloc(self, name, npart, free, dtype=F32, p0=0):
        esz = 4 if dtype in (F32, mybir.dt.int32) else 2
        nbytes = free * esz
        b0 = (self.cursor + 63) & ~63
        assert b0 + nbytes <= self.arena_bytes, \
            f"SBUF arena overflow at {name}: {(b0 + nbytes)//1024}KB > {ARENA_KB}KB"
        self.cursor = b0 + nbytes
        self.hwm = max(self.hwm, self.cursor)
        return Buf(self, name, dtype, p0, npart, b0, free)

    def mark(self):
        return self.cursor

    def release(self, mark):
        self.cursor = mark

    def bank(self, dtype=F32):
        i = self.bank_rr % 8
        self.bank_rr += 1
        return self.banks[i] if dtype == F32 else self.banks_bf[i]

    def _foot(self, ap):
        if isinstance(ap, Buf):
            ap = ap.ap()
        nm = ap.tensor.name
        if nm.startswith("bank"):
            return ("ps", int(nm[4:]))
        if nm == "arena":
            esz = {F32: 4, BF16: 2, mybir.dt.uint8: 1, mybir.dt.int32: 4}[ap.dtype]
            row = self.arena_bytes // esz
            off = ap.offset
            p0 = off // row
            f0 = off - p0 * row
            dims = list(ap.ap)
            step0, cnt0 = dims[0][0], dims[0][1]
            assert step0 == row or cnt0 == 1, f"partition step {step0} != {row}"
            span = 0
            for d in dims[1:]:
                st, cn = d[0], d[1]
                assert st >= 0
                span += (cn - 1) * st
            return ("sb", p0, p0 + cnt0, f0 * esz, (f0 + span + 1) * esz)
        return ("dram", nm)

    def _cells(self, foot):
        _, p0, p1, b0, b1 = foot
        for q in range(p0 // 32, (p1 + 31) // 32):
            base = q * self.npages
            for g in range(b0 // PAGE, (b1 + PAGE - 1) // PAGE):
                yield base + g

    def _dep_read(self, foot, deps):
        if foot[0] == "sb":
            for c in self._cells(foot):
                w = self.w_cell[c]
                if w is not None and w[1] > deps.get(w[0], 0):
                    deps[w[0]] = w[1]
        elif foot[0] == "ps":
            w = self.psum_w[foot[1]]
            if w is not None and w[1] > deps.get(w[0], 0):
                deps[w[0]] = w[1]
            lr = self.psum_lastr[foot[1]]
            if lr is not None and lr[1] > deps.get(lr[0], 0):
                deps[lr[0]] = lr[1]
        else:
            w = self.dram_w.get(foot[1])
            if w is not None and w[1] > deps.get(w[0], 0):
                deps[w[0]] = w[1]

    def _dep_write(self, foot, deps):
        self._dep_read(foot, deps)
        if foot[0] == "sb":
            for c in self._cells(foot):
                r = self.r_cell[c]
                if r:
                    for e, s in r.items():
                        if s > deps.get(e, 0):
                            deps[e] = s
        elif foot[0] == "ps":
            r = self.psum_r[foot[1]]
            if r:
                for e, s in r.items():
                    if s > deps.get(e, 0):
                        deps[e] = s
        else:
            r = self.dram_r.get(foot[1])
            if r:
                for e, s in r.items():
                    if s > deps.get(e, 0):
                        deps[e] = s

    def _rec_read(self, foot, eq, seq):
        if foot[0] == "sb":
            for c in self._cells(foot):
                r = self.r_cell[c]
                if r is None:
                    self.r_cell[c] = {eq: seq}
                else:
                    r[eq] = seq
        elif foot[0] == "ps":
            r = self.psum_r[foot[1]]
            if r is None:
                self.psum_r[foot[1]] = {eq: seq}
            else:
                r[eq] = seq
            self.psum_lastr[foot[1]] = (eq, seq)
        else:
            r = self.dram_r.get(foot[1])
            if r is None:
                self.dram_r[foot[1]] = {eq: seq}
            else:
                r[eq] = seq

    def _rec_write(self, foot, eq, seq):
        if foot[0] == "sb":
            for c in self._cells(foot):
                self.w_cell[c] = (eq, seq)
                self.r_cell[c] = None
        elif foot[0] == "ps":
            self.psum_w[foot[1]] = (eq, seq)
            self.psum_r[foot[1]] = None
            self.psum_lastr[foot[1]] = None
        else:
            self.dram_w[foot[1]] = (eq, seq)
            self.dram_r[foot[1]] = None

    def op(self, engine, fn, reads, writes, name="", dma=False):
        if dma:
            rr = self.dma_rr[engine]
            self.dma_rr[engine] = (rr + 1) % NSUB
            qname = f"dq_{engine}_{rr}"
        else:
            qname = engine
        eq = EIDX[qname]
        seq = len(self.q_ops[qname]) + 1
        rfoot = [self._foot(a) for a in reads]
        wfoot = [self._foot(a) for a in writes]
        deps = {}
        for f in rfoot:
            self._dep_read(f, deps)
        for f in wfoot:
            self._dep_write(f, deps)
        if dma and seq > 1:
            if deps.get(eq, 0) < seq - 1:
                deps[eq] = seq - 1
        if STRICT:
            for qe in ALLE:
                n = len(self.q_ops[qe])
                if n > 0:
                    deps[EIDX[qe]] = n
        op = Op(engine, fn, dma, qname, name)
        op.seq = seq
        vci = self.vc_now[engine]
        ei = EIDX[engine]
        for de in sorted(deps):
            ds = deps[de]
            if de == ei and not dma and not SAME_ENGINE_SYNC:
                continue
            if vci[de] >= ds:
                continue
            op.waits.append((de, ds))
            dvc = self.q_ops[ALLE[de]][ds - 1].vc
            for j in range(NE):
                if dvc[j] > vci[j]:
                    vci[j] = dvc[j]
            if ds > vci[de]:
                vci[de] = ds
        for f in rfoot:
            self._rec_read(f, eq, seq)
        for f in wfoot:
            self._rec_write(f, eq, seq)
        myvc = list(vci)
        if myvc[eq] < seq:
            myvc[eq] = seq
        op.vc = myvc
        self.q_ops[qname].append(op)
        self.streams[engine].append(op)
        return op

    def mm(self, out, lhsT, rhs, start, stop, name="mm"):
        def fn(t, out=out, lhsT=lhsT, rhs=rhs, start=start, stop=stop):
            return t.matmul(out, lhsT, rhs, start=start, stop=stop)
        return self.op("tensor", fn, [lhsT, rhs], [out], name=name)

    def trans(self, out, in_, ident, name="tr"):
        def fn(t, out=out, in_=in_, ident=ident):
            return t.transpose(out, in_, ident)
        return self.op("tensor", fn, [in_, ident], [out], name=name)

    def const_col(self, value):
        key = float(value)
        if key not in self.const_slots:
            i = len(self.const_slots)
            assert i < 64
            ap = self.constpool[:, i:i + 1]
            self.const_slots[key] = ap
            self.memset(ap, key, name=f"const_{i}")
        return self.const_slots[key]

    def act(self, out, in_, func, bias=0.0, scale=1.0, accum_out=None, name="act"):
        if isinstance(bias, float) and bias != 0.0:
            bias = self.const_col(bias)
        if isinstance(bias, bass.AP) and bias.shape[0] != out.shape[0]:
            bias = bias[0:out.shape[0], :]
        if isinstance(scale, bass.AP) and scale.shape[0] != out.shape[0]:
            scale = scale[0:out.shape[0], :]
        reads = [in_]
        writes = [out]
        if isinstance(bias, bass.AP):
            reads.append(bias)
        if isinstance(scale, bass.AP):
            reads.append(scale)
        if accum_out is not None:
            writes.append(accum_out)

        def fn(s, out=out, in_=in_, func=func, bias=bias, scale=scale, ao=accum_out):
            return s.activation(out, in_, func, bias=bias, scale=scale, accum_out=ao)
        return self.op("scalar", fn, reads, writes, name=name)

    def tt(self, out, a, b, aluop, eng="vector", name="tt"):
        def fn(e, out=out, a=a, b=b, aluop=aluop):
            return e.tensor_tensor(out, a, b, aluop)
        return self.op(eng, fn, [a, b], [out], name=name)

    def ts(self, out, in0, s1, s2, op0, op1=None, eng="vector", accum_out=None, name="ts"):
        if isinstance(s1, bass.AP) and s1.shape[0] != out.shape[0]:
            s1 = s1[0:out.shape[0], :]
        if isinstance(s2, bass.AP) and s2.shape[0] != out.shape[0]:
            s2 = s2[0:out.shape[0], :]
        reads = [in0] + [x for x in (s1, s2) if isinstance(x, bass.AP)]
        writes = [out] + ([accum_out] if accum_out is not None else [])

        def fn(e, out=out, in0=in0, s1=s1, s2=s2, op0=op0, op1=op1, ao=accum_out):
            if op1 is None:
                return e.tensor_scalar(out, in0, s1, None, op0, accum_out=ao)
            return e.tensor_scalar(out, in0, s1, s2, op0, op1, accum_out=ao)
        return self.op(eng, fn, reads, writes, name=name)

    def stt(self, out, in0, scal, in1, op0, op1, eng="vector", name="stt"):
        if isinstance(scal, bass.AP) and scal.shape[0] != out.shape[0]:
            scal = scal[0:out.shape[0], :]
        reads = [in0, in1] + ([scal] if isinstance(scal, bass.AP) else [])

        def fn(e, out=out, in0=in0, scal=scal, in1=in1, op0=op0, op1=op1):
            return e.scalar_tensor_tensor(out, in0, scal, in1, op0, op1)
        return self.op(eng, fn, reads, [out], name=name)

    def red(self, out, in_, aluop, eng="vector", negate=False, absval=False, name="red"):
        def fn(e, out=out, in_=in_, aluop=aluop, negate=negate, absval=absval):
            return e.tensor_reduce(out, in_, AX.X, aluop,
                                   apply_absolute_value=absval, negate=negate)
        return self.op(eng, fn, [in_], [out], name=name)

    def copy(self, out, in_, eng="vector", name="cp"):
        def fn(e, out=out, in_=in_):
            return e.tensor_copy(out, in_)
        return self.op(eng, fn, [in_], [out], name=name)

    def recip(self, out, in_, name="recip"):
        def fn(e, out=out, in_=in_):
            return e.reciprocal(out, in_)
        return self.op("vector", fn, [in_], [out], name=name)

    def memset(self, ap, val, eng="vector", name="memset"):
        def fn(e, ap=ap, val=val):
            return e.memset(ap, val)
        return self.op(eng, fn, [], [ap], name=name)

    def dma(self, queue, out, in_, transpose=False, name="dma"):
        def fn(e, out=out, in_=in_, transpose=transpose):
            if transpose:
                return e.dma_start(out, in_, transpose=True)
            return e.dma_start(out, in_)
        return self.op(queue, fn, [in_], [out], name=name, dma=True)

    def final_barrier(self):
        op = Op("sync", lambda e: e.nop(), False, "sync", "final")
        op.seq = len(self.q_ops["sync"]) + 1
        op.vc = [0] * NE
        for q in DMAQ:
            for r in range(NSUB):
                qn = f"dq_{q}_{r}"
                n = len(self.q_ops[qn])
                if n > 0:
                    op.waits.append((EIDX[qn], n))
        self.q_ops["sync"].append(op)
        self.streams["sync"].append(op)

    def emit(self, ctx):
        nc = self.nc
        sems = {e: ctx.enter_context(nc.semaphore("sem_" + e)) for e in ALLE}
        needed = {e: set() for e in ALLE}
        last_dma = {}
        for e in ENG:
            for op in self.streams[e]:
                for (de, ds) in op.waits:
                    needed[ALLE[de]].add(ds)
        semval = {}
        for e in ALLE:
            m = [0] * (len(self.q_ops[e]) + 1)
            c = 0
            isdma = e.startswith("dq_")
            for i, op in enumerate(self.q_ops[e], start=1):
                if isdma or i in needed[e]:
                    c += 16 if isdma else 1
                    op.inc = True
                m[i] = c
            semval[e] = m
        block = ctx.enter_context(nc.Block())

        def mkbody(ename):
            stream = self.streams[ename]

            def body(eng):
                for op in stream:
                    for (de, ds) in op.waits:
                        dep_e = ALLE[de]
                        v = semval[dep_e][ds]
                        assert v > 0, f"wait on non-inc op {dep_e}@{ds} from {op.name}"
                        eng.wait_ge(sems[dep_e], v)
                    ins = op.fn(eng)
                    if op.inc:
                        ins.then_inc(sems[op.queue], 16 if op.is_dma else 1)
            return body

        for ename in ENG:
            if self.streams[ename]:
                getattr(block, ename)(mkbody(ename))
        return sems, semval



def make_constants():
    c = {}
    c["ident_bf"] = np.eye(128, dtype=ml_dtypes.bfloat16)
    c["ident_f32"] = np.eye(128, dtype=np.float32)
    c["ones_bf"] = np.ones((128, 128), dtype=ml_dtypes.bfloat16)
    j = (np.arange(128) % 64) // 2
    invf = 10000.0 ** (-2.0 * j / 64.0)
    t = np.arange(S, dtype=np.float64)
    ang = invf[:, None] * t[None, :]
    c2 = np.cos(ang)
    s2 = np.sin(ang)
    sgn = np.where(np.arange(128) % 2 == 0, -1.0, 1.0)
    c["rope0_c"] = c2.astype(ml_dtypes.bfloat16)
    c["rope0_s"] = (s2 * sgn[:, None]).astype(ml_dtypes.bfloat16)
    psw = np.zeros((128, 128), dtype=np.float32)
    for m_ in range(128):
        sw = m_ + 1 if m_ % 2 == 0 else m_ - 1
        psw[sw, m_] = 1.0
    c["pswap"] = psw.astype(ml_dtypes.bfloat16)
    rs = np.zeros((4, 4 * 128), dtype=np.float32)
    for a in range(4):
        rs[a, a * 128:(a + 1) * 128] = 1.0
    c["rowsel4"] = rs.astype(ml_dtypes.bfloat16)
    G = np.zeros((128, 32 * 128), dtype=np.float32)
    for j_ in range(4):
        for half in range(2):
            for cl in range(4):
                idx = (j_ * 2 + half) * 4 + cl
                for m_ in range(128):
                    d1 = 512 * half + 4 * m_ + j_
                    if d1 // 128 == 4 * half + cl:
                        G[d1 % 128, idx * 128 + m_] = 1.0
    c["l0f_G"] = G.astype(ml_dtypes.bfloat16)
    sel = np.zeros((128, 4 * 128), dtype=np.float32)
    for km in range(4):
        for m_ in range(128):
            sel[km * 32 + (m_ % 32), km * 128 + m_] = 1.0
    c["l2_selrep"] = sel.astype(ml_dtypes.bfloat16)
    g4 = np.zeros((128, 4 * 16), dtype=np.float32)
    for ci in range(4):
        for p in range(128):
            g4[p, ci * 16 + 4 * ci + p // 32] = 1.0
    c["l2_G4"] = g4.astype(ml_dtypes.bfloat16)
    brd = np.zeros((16, 8 * 128), dtype=np.float32)
    for cc in range(8):
        for m_ in range(128):
            brd[2 * cc + m_ // 64, cc * 128 + m_] = 1.0
    c["l2_brd"] = brd.astype(ml_dtypes.bfloat16)
    sv = np.zeros((128, 2 * 128), dtype=np.float32)
    for par in range(2):
        for m_ in range(128):
            sv[64 * par + (m_ % 64), par * 128 + m_] = 1.0
    c["l2_selv"] = sv.astype(ml_dtypes.bfloat16)
    bd = np.zeros((128, 128), dtype=np.float32)
    for p in range(128):
        for m_ in range(128):
            if p // 64 == m_ // 64:
                bd[p, m_] = 1.0
    c["bd64"] = bd.astype(ml_dtypes.bfloat16)
    cm = np.zeros((128, 512), dtype=np.float32)
    for dkv in range(2):
        for rk in range(128):
            for cq in range(256):
                if dkv * 128 + rk > cq:
                    cm[rk, dkv * 256 + cq] = -1e30
    c["l1_causmask"] = cm.astype(ml_dtypes.bfloat16)
    dh = np.arange(128) % 64
    jj = dh % 32
    invf = 1.0 / 10000.0 ** (2.0 * jj / 64.0)
    t = np.arange(S, dtype=np.float64)
    fr = t[None, :] * invf[:, None]
    base = (2.0 * jj + 0.4 * 64) / (1.4 * 64)
    power = (t - S // 2) / (256 // 2)
    xsc = base[:, None] ** power[None, :]
    c["l1f_cosscl"] = (np.cos(fr) * xsc).astype(ml_dtypes.bfloat16)
    c["l1f_sinscl"] = (np.sin(fr) * xsc).astype(ml_dtypes.bfloat16)
    c["l1f_cosinv"] = (np.cos(fr) / xsc).astype(ml_dtypes.bfloat16)
    c["l1f_sininv"] = (np.sin(fr) / xsc).astype(ml_dtypes.bfloat16)
    pr = np.zeros((128, 128), dtype=np.float32)
    for m_ in range(128):
        g_ = m_ // 64
        lm = m_ % 64
        if lm < 32:
            pr[g_ * 64 + lm + 32, m_] = -1.0
        else:
            pr[g_ * 64 + lm - 32, m_] = 1.0
    c["pr64"] = pr.astype(ml_dtypes.bfloat16)
    wm = np.zeros((128, 3 * 384), dtype=np.float32)
    for mv in range(3):
        for qr in range(128):
            for kr in range(384):
                ok = (qr <= kr <= qr + 256)
                if mv == 0 and kr < 256:
                    ok = False
                if mv == 1 and kr < 128:
                    ok = False
                if not ok:
                    wm[qr, mv * 384 + kr] = -1e30
    c["l1f_winmask"] = wm.astype(ml_dtypes.bfloat16)
    tr = np.zeros((128, 128), dtype=np.float32)
    for s_ in range(128):
        for l_ in range(128):
            if s_ <= l_:
                tr[s_, l_] = 1.0
    c["trilf"] = tr
    rs = np.zeros((32, 32 * 128), dtype=np.float32)
    for h in range(32):
        rs[h, h * 128:(h + 1) * 128] = 1.0
    c["rowsel32"] = rs.astype(ml_dtypes.bfloat16)
    sm = np.zeros((128, 128), dtype=np.float32)
    for s_ in range(128):
        for l_ in range(128):
            if l_ < s_:
                sm[s_, l_] = -1e30
    c["l3_smask"] = sm.astype(ml_dtypes.bfloat16)
    c["rowsel32f"] = rs
    return c


CONSTS = make_constants()



def ln_tb(mk, tb, dst, gamma_col=None, beta_col=None, dst_cols_per_chunk=TB,
          src_fn=None, src_bf16=False, dst_t0=0):
    k = mk.k
    t0 = tb * TB
    if src_fn is None:
        src_fn = mk.rT_ap
    sc = mk.ln_scr
    x2b = sc["x2b"]
    ps_sum = k.banks[6]
    ps_sq = k.banks[7]
    for c in range(NCHUNK):
        xb = dst[:, c * dst_cols_per_chunk + dst_t0:c * dst_cols_per_chunk + dst_t0 + TB]
        src = src_fn(c, t0, t0 + TB)
        if src_bf16:
            stats_in = src
        else:
            k.copy(xb, src, eng="vector", name=f"lncast{c}")
            stats_in = xb
        k.act(x2b.ap(), src, AF.Square, name=f"lnsq{c}")
        k.mm(ps_sum[:, 0:TB], mk.oneb.ap(), stats_in, start=(c == 0),
             stop=(c == NCHUNK - 1), name=f"lnsum{c}")
        k.mm(ps_sq[:, 0:TB], mk.oneb.ap(), x2b.ap(), start=(c == 0), stop=(c == NCHUNK - 1),
             name=f"lnsumsq{c}")
    mu = sc["mu"]
    rstd = sc["rstd"]
    tmp = sc["tmp"]
    k.ts(mu.ap(), ps_sum[:, 0:TB], 1.0 / D, None, ALU.mult, name="ln_mu")
    k.ts(tmp.ap(), ps_sq[:, 0:TB], 1.0 / D, 1e-5, ALU.mult, ALU.add, name="ln_msq")
    k.act(rstd.ap(), mu.ap(), AF.Square, name="ln_mu2")
    k.tt(tmp.ap(), tmp.ap(), rstd.ap(), ALU.subtract, name="ln_var")
    k.act(rstd.ap(), tmp.ap(), AF.Sqrt, name="ln_sqrt")
    k.recip(rstd.ap(), rstd.ap(), name="ln_rstd")
    for c in range(NCHUNK):
        xb = dst[:, c * dst_cols_per_chunk + dst_t0:c * dst_cols_per_chunk + dst_t0 + TB]
        k.tt(tmp.ap(), src_fn(c, t0, t0 + TB), mu.ap(), ALU.subtract, name=f"lnsub{c}")
        if gamma_col is None:
            k.tt(xb, tmp.ap(), rstd.ap(), ALU.mult, name=f"lnmul{c}")
        else:
            k.tt(tmp.ap(), tmp.ap(), rstd.ap(), ALU.mult, name=f"lnmul{c}")
            k.ts(xb, tmp.ap(), gamma_col[:, c:c + 1], beta_col[:, c:c + 1],
                 ALU.mult, ALU.add, name=f"lngb{c}")


def alloc_ln_scratch(mk):
    k = mk.k
    mk.ln_scr = {
        "x2b": k.alloc("ln_x2b", 128, TB, BF16),
        "mu": k.alloc("ln_mu", 128, TB, F32),
        "rstd": k.alloc("ln_rstd", 128, TB, F32),
        "tmp": k.alloc("ln_tmp", 128, TB, F32),
    }


def wT_scratch(mk, name, dram_ap, R, C):
    scr = mk.nc.dram_tensor("scr_" + name, [R, C], BF16, kind="Internal").ap()
    mk.k.dma("gpsimd", scr, dram_ap, name="cast_" + name)
    return scr


def load_wT_chunks(mk, buf, scr, R, C, cc0, ncc, queue="sync"):
    k = mk.k
    for i in range(ncc):
        cc = cc0 + i
        k.dma(queue, buf[:, i * R:(i + 1) * R], scr[:, cc * 128:(cc + 1) * 128],
              transpose=True, name=f"wT{cc}")


def load_wN(mk, name, dram_ap, K, N):
    k = mk.k
    buf = k.alloc(name, 128, (K // 128) * N, BF16)
    out3 = buf.ap().rearrange("p (c n) -> p c n", n=N)
    in3 = dram_ap.rearrange("(c p) n -> p c n", p=128)
    k.dma("gpsimd", out3, in3, name="ldw_" + name)
    return buf



class MK:

    def __init__(self, nc, modules):
        self.nc = nc
        self.k = Kern(nc)
        self.modules = modules
        self.inp = {}
        self.dumps = []
        self.dbg = {}
        self._dumped = set()

    def dump(self, name, buf):
        if name not in self.dumps or name in self._dumped:
            return
        self._dumped.add(name)
        dap = self.nc.dram_tensor("dump_" + name, [buf.npart, buf.free],
                                  buf.dtype, kind="ExternalOutput").ap()
        self.k.dma("gpsimd", dap, buf.ap(), name="dump_" + name)

    def dram_in(self, name, shape, dtype=F32):
        if name in self.inp:
            return self.inp[name]
        ap = self.nc.dram_tensor(name, list(shape), dtype, kind="ExternalInput").ap()
        self.inp[name] = ap
        return ap


def build_program(modules, dumps=()):
    nc = bass.Bass("TRN2", target_bir_lowering=False)
    mk = MK(nc, modules)
    mk.dumps = list(dumps)
    k = mk.k

    x_in = mk.dram_in("x", [S, D])
    y_out = nc.dram_tensor("y", [S, D], F32, kind="ExternalOutput").ap()
    ident_f32 = mk.dram_in("ident_f32", [128, 128])
    ident_bf = mk.dram_in("ident_bf", [128, 128], BF16)
    ones_bf = mk.dram_in("ones_bf", [128, 128], BF16)

    k.constpool = k.alloc("constpool", 128, 64, F32)
    rT = k.alloc("rT", 128, NCHUNK * S, F32)

    def rT_ap(c, t0=0, t1=S):
        return rT[:, c * S + t0: c * S + t1]

    idf = k.alloc("idf", 128, 128, F32)
    idb = k.alloc("idb", 128, 128, BF16)
    oneb = k.alloc("oneb", 128, 128, BF16)
    k.dma("sync", idf.ap(), ident_f32, name="ld_idf")
    k.dma("sync", idb.ap(), ident_bf, name="ld_idb")
    k.dma("sync", oneb.ap(), ones_bf, name="ld_ones")
    mk.idf, mk.idb, mk.oneb = idf, idb, oneb
    mk.rT, mk.rT_ap = rT, rT_ap

    _m0 = k.mark()
    stage = [k.alloc(f"stage{i}", 128, D, F32) for i in range(2)]
    for t in range(NTILE):
        st = stage[t % 2]
        k.dma("sync", st.ap(), x_in[t * 128:(t + 1) * 128, :], name=f"ldx{t}")
        for c in range(NCHUNK):
            pb = k.bank()
            pbv = pb.ap()[:, 0:128]
            k.trans(pbv, st[:, c * 128:(c + 1) * 128], idf.ap(), name=f"trx{t}_{c}")
            k.copy(rT_ap(c, t * 128, (t + 1) * 128), pbv, eng="vector", name=f"evx{t}_{c}")

    k.release(_m0)

    for m in modules:
        marker = k.mark()
        MODULE_EMITTERS[m](mk)
        k.release(marker)
        mk.dump("rt_" + m, rT)

    ostage = [k.alloc(f"ostage{i}", 128, D, F32) for i in range(2)]
    for t in range(NTILE):
        ot = ostage[t % 2]
        for c in range(NCHUNK):
            pb = k.bank()
            pbv = pb.ap()[:, 0:128]
            k.trans(pbv, rT_ap(c, t * 128, (t + 1) * 128), idf.ap(), name=f"otr{t}_{c}")
            k.copy(ot[:, c * 128:(c + 1) * 128], pbv, eng="vector", name=f"oev{t}_{c}")
        k.dma("sync", y_out[t * 128:(t + 1) * 128, :], ot.ap(), name=f"stx{t}")

    k.final_barrier()
    with ExitStack() as ctx:
        k.emit(ctx)
    return nc, list(mk.inp.keys())



MODULE_EMITTERS = {}


def emit_L0_attn(mk):
    k, nc = mk.k, mk.nc
    wq_d = mk.dram_in("L0_attn_wq", [D, D])
    wk_d = mk.dram_in("L0_attn_wk", [D, D])
    wv_d = mk.dram_in("L0_attn_wv", [D, D])
    c2_d = mk.dram_in("rope0_c", [128, S], BF16)
    s2_d = mk.dram_in("rope0_s", [128, S], BF16)
    psw_d = mk.dram_in("pswap", [128, 128], BF16)

    qT = k.alloc("l0_qT", 128, NCHUNK * S, BF16)
    kTb = k.alloc("l0_kT", 128, NCHUNK * S, BF16)
    vtok = k.alloc("l0_vtok", 128, NTILE * 16 * 65, BF16)
    c2 = k.alloc("l0_c2", 128, S, BF16)
    s2 = k.alloc("l0_s2", 128, S, BF16)
    psw = k.alloc("l0_psw", 128, 128, BF16)
    k.dma("sync", c2.ap(), c2_d, name="ld_c2")
    k.dma("sync", s2.ap(), s2_d, name="ld_s2")
    k.dma("sync", psw.ap(), psw_d, name="ld_psw")
    alloc_ln_scratch(mk)
    xn = k.alloc("l0_xn", 128, NCHUNK * TB, BF16)
    wbuf = k.alloc("l0_w", 128, NCHUNK * D, BF16)
    qsc = k.alloc("l0_qsc", 128, TB, BF16)
    ebuf = [k.alloc(f"l0_e{i}", 128, TB, BF16) for i in range(2)]
    rec = k.alloc("l0_rec", 128, 1, F32)
    asb = k.alloc("l0_asb", 128, 64, BF16)

    scr_q = wT_scratch(mk, "l0q", wq_d, D, D)
    scr_k = wT_scratch(mk, "l0k", wk_d, D, D)
    scr_v = wT_scratch(mk, "l0v", wv_d, D, D)

    def vt(t):
        return vtok[:, t * 16 * 65:(t + 1) * 16 * 65]

    for t in range(NTILE):
        k.memset(vt(t).rearrange("p (h c) -> p h c", c=65)[:, :, 64:65], 1.0,
                 name=f"vones{t}")

    for tb in range(NTB):
        t0 = tb * TB
        ln_tb(mk, tb, xn)

        def xn_c(c, a=0, b_=TB):
            return xn[:, c * TB + a:c * TB + b_]
        load_wT_chunks(mk, wbuf, scr_v, D, D, 0, NCHUNK, queue="sync")
        for tt_ in range(4):
            t128 = tb * 4 + tt_
            for fh in range(2):
                pb = k.banks[fh]
                for c in range(NCHUNK):
                    k.mm(pb[:, 0:TB], xn_c(c, tt_ * 128, tt_ * 128 + 128),
                         wbuf[:, c * D + fh * TB:c * D + fh * TB + TB],
                         start=(c == 0), stop=(c == NCHUNK - 1), name=f"vp{t128}_{fh}")
                dst = vt(t128).rearrange("p (h c) -> p h c", c=65)[:, fh * 8:fh * 8 + 8, 0:64]
                k.copy(dst, pb[:, 0:TB].rearrange("p (h c) -> p h c", c=64),
                       eng="vector", name=f"vev{t128}_{fh}")
        load_wT_chunks(mk, wbuf, scr_k, D, D, 0, NCHUNK, queue="sync")
        for co in range(NCHUNK):
            pb = k.banks[2 + (co % 2)]
            for ci in range(NCHUNK):
                k.mm(pb[:, 0:TB], wbuf[:, ci * D + co * 128:ci * D + co * 128 + 128],
                     xn_c(ci), start=(ci == 0), stop=(ci == NCHUNK - 1), name=f"kp{co}")
            k.copy(kTb[:, co * S + t0:co * S + t0 + TB], pb[:, 0:TB], eng="vector",
                   name=f"kev{co}")
        load_wT_chunks(mk, wbuf, scr_q, D, D, 0, NCHUNK, queue="sync")
        for co in range(NCHUNK):
            pb = k.banks[2 + (co % 2)]
            for ci in range(NCHUNK):
                k.mm(pb[:, 0:TB], wbuf[:, ci * D + co * 128:ci * D + co * 128 + 128],
                     xn_c(ci), start=(ci == 0), stop=(ci == NCHUNK - 1), name=f"qp{co}")
            k.copy(qsc.ap(), pb[:, 0:TB], eng="vector", name=f"qev{co}")
            pb2 = k.banks[4 + (co % 2)]
            k.mm(pb2[:, 0:TB], psw.ap(), qsc.ap(), start=True, stop=True, name=f"qsw{co}")
            dst = qT[:, co * S + t0:co * S + t0 + TB]
            k.tt(dst, qsc.ap(), c2[:, t0:t0 + TB], ALU.mult, name=f"qr1{co}")
            k.tt(qsc.ap(), pb2[:, 0:TB], s2[:, t0:t0 + TB], ALU.mult, name=f"qr2{co}")
            k.tt(dst, dst, qsc.ap(), ALU.add, name=f"qr3{co}")
    for co in range(NCHUNK):
        for tb in range(NTB):
            t0 = tb * TB
            src = kTb[:, co * S + t0:co * S + t0 + TB]
            pb2 = k.banks[4 + (tb % 2)]
            k.mm(pb2[:, 0:TB], psw.ap(), src, start=True, stop=True, name=f"ksw{co}")
            k.tt(src, src, c2[:, t0:t0 + TB], ALU.mult, name=f"kr1{co}")
            k.tt(qsc.ap(), pb2[:, 0:TB], s2[:, t0:t0 + TB], ALU.mult, name=f"kr2{co}")
            k.tt(src, src, qsc.ap(), ALU.add, name=f"kr3{co}")

    for h in range(16):
        ch, r0 = h // 2, 64 * (h % 2)
        for qb in range(NTB):
            q0 = qb * TB
            for kt in range(NTILE):
                ps = k.banks[4 + (kt % 2)]
                k.mm(ps[:, 0:TB],
                     kTb[r0:r0 + 64, ch * S + kt * 128:ch * S + kt * 128 + 128],
                     qT[r0:r0 + 64, ch * S + q0:ch * S + q0 + TB],
                     start=True, stop=True, name=f"qk{h}_{qb}_{kt}")
                k.act(ebuf[kt % 2].ap(), ps[:, 0:TB], AF.Exp, bias=-20.0,
                      scale=0.125, name=f"exp{h}_{qb}_{kt}")
                for i in range(4):
                    k.mm(k.banks[i][:, 0:65], ebuf[kt % 2][:, i * 128:(i + 1) * 128],
                         vt(kt)[:, h * 65:h * 65 + 65],
                         start=(kt == 0), stop=(kt == NTILE - 1), name=f"pv{h}_{qb}_{i}")
            for i in range(4):
                po = k.banks[i]
                k.recip(rec.ap(), po[:, 64:65], name=f"rc{h}_{qb}_{i}")
                k.ts(asb.ap(), po[:, 0:64], rec[:, 0:1], None, ALU.mult,
                     name=f"nrm{h}_{qb}_{i}")
                pt = k.banks_bf[6 + (i % 2)]
                k.trans(pt[0:64, 0:128], asb.ap(), mk.idb.ap(), name=f"atr{h}_{qb}_{i}")
                dst = mk.rT_ap(ch, q0 + i * 128, q0 + (i + 1) * 128)
                k.tt(dst[r0:r0 + 64, :] if r0 else dst[0:64, :],
                     dst[r0:r0 + 64, :] if r0 else dst[0:64, :],
                     pt[0:64, 0:128], ALU.add, name=f"radd{h}_{qb}_{i}")


MODULE_EMITTERS["L0_attn"] = emit_L0_attn


def col_chunks_from_row(mk, row_ap, dst, nch, bank=7, scale_col=None):
    k = mk.k
    one = k.const_col(1.0)
    for ch in range(nch):
        pb = k.banks[bank]
        k.mm(pb[:, 0:1], row_ap[0:1, ch * 128:(ch + 1) * 128], one[0:1, 0:1],
             start=True, stop=True, name=f"c2r{ch}")
        if scale_col is None:
            k.copy(dst[:, ch:ch + 1], pb[:, 0:1], eng="vector", name=f"c2rc{ch}")
        else:
            k.ts(dst[:, ch:ch + 1], pb[:, 0:1], scale_col, None, ALU.mult,
                 name=f"c2rc{ch}")


def load_row(mk, name, dram_ap, n):
    k = mk.k
    buf = k.alloc(name, 1, n, F32)
    k.dma("gpsimd", buf.ap(), dram_ap.rearrange("(a n) -> a n", a=1), name="ldr_" + name)
    return buf


def emit_L0_ffn(mk):
    k, nc = mk.k, mk.nc
    H = 4096
    w1_d = mk.dram_in("L0_ffn_W1", [H, D])
    b1_d = mk.dram_in("L0_ffn_b1", [H])
    w2_d = mk.dram_in("L0_ffn_W2", [D, H])
    b2_d = mk.dram_in("L0_ffn_b2", [D])
    sc_d = mk.dram_in("L0_ffn_scale", [1])
    rs4_d = mk.dram_in("rowsel4", [4, 512], BF16)
    g_d = mk.dram_in("l0f_G", [128, 32 * 128], BF16)

    scr1 = wT_scratch(mk, "l0f1", w1_d, H, D)
    scr2 = wT_scratch(mk, "l0f2", w2_d, D, H)

    alloc_ln_scratch(mk)
    xn = k.alloc("f0_xn", 128, NCHUNK * TB, BF16)
    w1b = k.alloc("f0_w1", 128, NCHUNK * 1024, BF16)
    w2b = k.alloc("f0_w2", 128, 8 * 1024, BF16)
    hg = k.alloc("f0_hg", 128, 8 * TB, BF16)
    outac = k.alloc("f0_out", 128, NCHUNK * TB, F32)
    lorac = k.alloc("f0_lc", 128, NCHUNK * 16, BF16)
    rs4 = k.alloc("f0_rs4", 4, 512, BF16)
    gbuf = k.alloc("f0_G", 128, 32 * 128, BF16)
    b1c = k.alloc("f0_b1c", 128, 32, F32)
    b2c = k.alloc("f0_b2c", 128, NCHUNK, F32)
    ec = k.alloc("f0_ec", 128, 32, F32)
    ecol = k.alloc("f0_ecol", 128, 32, F32)
    sclb = k.alloc("f0_scl", 128, 1, F32)
    u1b = k.alloc("f0_u1b", 128, 4 * TB, F32)
    u1s = k.alloc("f0_u1s", 4, TB, BF16)
    u2s = [k.alloc(f"f0_u2s{j}", 4, TB, BF16) for j in range(4)]
    pre1 = k.alloc("f0_pre1", 128, TB, F32)
    onesr = k.alloc("f0_onesr", 1, 128, F32)
    sclsb = k.alloc("f0_sclsb", 1, 1, F32)

    k.dma("sync", rs4.ap(), rs4_d, name="ld_rs4")
    k.dma("sync", gbuf.ap(), g_d, name="ld_G")
    k.dma("gpsimd", sclsb.ap(), sc_d.rearrange("(a n) -> a n", a=1), name="ld_scl")
    k.memset(onesr.ap(), 1.0)
    b1row = load_row(mk, "f0_b1row", b1_d, H)
    b2row = load_row(mk, "f0_b2row", b2_d, D)
    pb = k.banks[7]
    k.mm(pb[:, 0:1], onesr.ap(), sclsb.ap(), start=True, stop=True, name="sclbc")
    k.copy(sclb.ap(), pb[:, 0:1], eng="vector", name="sclbc2")
    col_chunks_from_row(mk, b1row.ap(), b1c, 32)
    col_chunks_from_row(mk, b2row.ap(), b2c, NCHUNK)

    for tb in range(NTB):
        t0 = tb * TB
        ln_tb(mk, tb, xn)

        def xn_c(c):
            return xn[:, c * TB:(c + 1) * TB]
        if tb == 0:
            for c in range(NCHUNK):
                k.copy(lorac[:, c * 16:(c + 1) * 16], xn[:, c * TB + 1:c * TB + 17],
                       eng="vector", name=f"lsave{c}")
            for ch in range(32):
                a, csub = ch // 8, ch % 8
                k.tt(ec[:, ch:ch + 1], lorac[:, csub * 16 + 2 * a:csub * 16 + 2 * a + 1],
                     sclb[:, 0:1], ALU.mult, name=f"ec{ch}")
            for co in range(NCHUNK):
                a, half = co // 2, co % 2
                for j_ in range(4):
                    pb = k.banks[6]
                    for cl in range(4):
                        gidx = (j_ * 2 + half) * 4 + cl
                        csrc = 4 * half + cl
                        k.mm(pb[:, 0:1], gbuf[:, gidx * 128:(gidx + 1) * 128],
                             lorac[:, csrc * 16 + 8 + 2 * a:csrc * 16 + 9 + 2 * a],
                             start=(cl == 0), stop=(cl == 3), name=f"ecol{co}_{j_}")
                    k.ts(ecol[:, co * 4 + j_:co * 4 + j_ + 1], pb[:, 0:1],
                         sclb[:, 0:1], None, ALU.mult, name=f"ecolc{co}_{j_}")
        pbu = k.banks[6]
        for c in range(NCHUNK):
            k.mm(pbu[:4, 0:TB], lorac[:, c * 16 + 1:c * 16 + 8:2], xn_c(c),
                 start=(c == 0), stop=(c == NCHUNK - 1), name=f"u1_{c}")
        k.copy(u1s.ap(), pbu[:4, 0:TB], eng="vector", name="u1ev")
        for a in range(4):
            pbb = k.banks[7]
            k.mm(pbb[:, 0:TB], rs4[:, a * 128:(a + 1) * 128], u1s.ap(),
                 start=True, stop=True, name=f"u1b{a}")
            k.copy(u1b[:, a * TB:(a + 1) * TB], pbb[:, 0:TB], eng="vector",
                   name=f"u1bc{a}")
        for hb in range(4):
            for ci in range(NCHUNK):
                k.dma("sync", w1b[:, ci * 1024:(ci + 1) * 1024],
                      scr1[hb * 1024:(hb + 1) * 1024, ci * 128:(ci + 1) * 128],
                      transpose=True, name=f"w1T{hb}_{ci}")
                k.dma("sync", w2b[:, ci * 1024:(ci + 1) * 1024],
                      scr2[:, (hb * 8 + ci) * 128:(hb * 8 + ci + 1) * 128],
                      transpose=True, name=f"w2T{hb}_{ci}")
            for co in range(8):
                ch = hb * 8 + co
                a = ch // 8
                pbh = k.banks[co % 2]
                for ci in range(NCHUNK):
                    k.mm(pbh[:, 0:TB], w1b[:, ci * 1024 + co * 128:ci * 1024 + co * 128 + 128],
                         xn_c(ci), start=(ci == 0), stop=(ci == NCHUNK - 1),
                         name=f"w1mm{ch}")
                if os.environ.get("MK_NOLORA"):
                    k.copy(pre1.ap(), pbh[:, 0:TB], eng="vector", name=f"cp1_{ch}")
                else:
                    k.stt(pre1.ap(), u1b[:, a * TB:(a + 1) * TB], ec[:, ch:ch + 1],
                          pbh[:, 0:TB], ALU.mult, ALU.add, name=f"lora1_{ch}")
                k.act(hg[:, co * TB:(co + 1) * TB], pre1.ap(), AF.Gelu,
                      bias=b1c[:, ch:ch + 1], name=f"gelu{ch}")
            pbu2 = k.banks[6]
            for ci in range(NCHUNK):
                k.mm(pbu2[:4, 0:TB], lorac[:, ci * 16 + 9:ci * 16 + 16:2],
                     hg[:, ci * TB:(ci + 1) * TB],
                     start=(ci == 0), stop=(ci == NCHUNK - 1), name=f"u2_{hb}_{ci}")
            k.copy(u2s[hb].ap(), pbu2[:4, 0:TB], eng="vector", name=f"u2ev{hb}")
            for co in range(NCHUNK):
                pbo = k.banks[2 + co % 2]
                for ci in range(NCHUNK):
                    k.mm(pbo[:, 0:TB], w2b[:, ci * 1024 + co * 128:ci * 1024 + co * 128 + 128],
                         hg[:, ci * TB:(ci + 1) * TB],
                         start=(ci == 0), stop=(ci == NCHUNK - 1), name=f"w2mm{hb}_{co}")
                dst = outac[:, co * TB:(co + 1) * TB]
                if hb == 0:
                    k.copy(dst, pbo[:, 0:TB], eng="vector", name=f"oac{hb}_{co}")
                else:
                    k.tt(dst, dst, pbo[:, 0:TB], ALU.add, name=f"oac{hb}_{co}")
        for j_ in range(4) if not os.environ.get("MK_NOLORA") else []:
            for a in range(4):
                pbb = k.banks[4 + (a % 2)]
                k.mm(pbb[:, 0:TB], rs4[:, a * 128:(a + 1) * 128], u2s[j_].ap(),
                     start=True, stop=True, name=f"u2b{j_}_{a}")
                for half in range(2):
                    co = 2 * a + half
                    k.stt(outac[:, co * TB:(co + 1) * TB], pbb[:, 0:TB],
                          ecol[:, co * 4 + j_:co * 4 + j_ + 1],
                          outac[:, co * TB:(co + 1) * TB], ALU.mult, ALU.add,
                          name=f"lora2_{j_}_{co}")
        for co in range(NCHUNK):
            k.ts(outac[:, co * TB:(co + 1) * TB], outac[:, co * TB:(co + 1) * TB],
                 b2c[:, co:co + 1], None, ALU.add, name=f"b2add{co}")
            dst = mk.rT_ap(co, t0, t0 + TB)
            k.tt(dst, dst, outac[:, co * TB:(co + 1) * TB], ALU.add, name=f"radd{co}")


MODULE_EMITTERS["L0_ffn"] = emit_L0_ffn


def lam_broadcast(mk, pref, q1, k1, q2, k2, lam_init, nvec):
    k = mk.k
    r1 = load_row(mk, pref + "q1", q1, nvec)
    r2 = load_row(mk, pref + "k1", k1, nvec)
    r3 = load_row(mk, pref + "q2", q2, nvec)
    r4 = load_row(mk, pref + "k2", k2, nvec)
    t1 = k.alloc(pref + "t1", 1, nvec, F32)
    s1 = k.alloc(pref + "s1", 1, 4, F32)
    onesr = k.alloc(pref + "or", 1, 128, F32)
    lamb = k.alloc(pref + "lam", 128, 1, F32)
    k.memset(onesr.ap(), 1.0)
    k.tt(t1.ap(), r1.ap(), r2.ap(), ALU.mult, name="lm1")
    k.red(s1[:, 0:1], t1.ap(), ALU.add, name="lr1")
    k.tt(t1.ap(), r3.ap(), r4.ap(), ALU.mult, name="lm2")
    k.red(s1[:, 1:2], t1.ap(), ALU.add, name="lr2")
    k.act(s1[:, 2:3], s1[:, 0:1], AF.Exp, name="le1")
    k.act(s1[:, 3:4], s1[:, 1:2], AF.Exp, name="le2")
    k.tt(s1[:, 2:3], s1[:, 2:3], s1[:, 3:4], ALU.subtract, name="lsub")
    k.ts(s1[:, 2:3], s1[:, 2:3], float(lam_init), None, ALU.add, name="lini")
    pb = k.banks[7]
    k.mm(pb[:, 0:1], onesr.ap(), s1[:, 2:3], start=True, stop=True, name="lbc")
    k.copy(lamb.ap(), pb[:, 0:1], eng="vector", name="lbc2")
    return lamb


def emit_L2_attn(mk):
    k, nc = mk.k, mk.nc
    import math
    LAM_INIT = 0.8 - 0.6 * math.exp(-0.3)
    H = 4096
    wq_d = mk.dram_in("L2_attn_Wq", [D, D])
    wk_d = mk.dram_in("L2_attn_Wk", [D, D])
    wv_d = mk.dram_in("L2_attn_Wv", [D, D])
    wo_d = mk.dram_in("L2_attn_Wo", [D, D])
    bo_d = mk.dram_in("L2_attn_bo", [D])
    g1_d = mk.dram_in("L2_attn_ln1_g", [D])
    b1l_d = mk.dram_in("L2_attn_ln1_b", [D])
    g2_d = mk.dram_in("L2_attn_ln2_g", [D])
    b2l_d = mk.dram_in("L2_attn_ln2_b", [D])
    w1_d = mk.dram_in("L2_attn_W1", [H, D])
    bf1_d = mk.dram_in("L2_attn_b1", [H])
    w2_d = mk.dram_in("L2_attn_W2", [D, H])
    bf2_d = mk.dram_in("L2_attn_b2", [D])
    sel_d = mk.dram_in("l2_selrep", [128, 512], BF16)
    g4_d = mk.dram_in("l2_G4", [128, 64], BF16)
    brd_d = mk.dram_in("l2_brd", [16, 1024], BF16)
    selv_d = mk.dram_in("l2_selv", [128, 256], BF16)
    bd_d = mk.dram_in("bd64", [128, 128], BF16)

    scr_q = nc.dram_tensor("scr_l2q", [D, D], BF16, kind="Internal").ap()
    k.dma("gpsimd", scr_q.rearrange("(p hq dd) d -> p hq dd d", p=2, hq=16),
          wq_d.rearrange("(hq dd p) d -> p hq dd d", p=2, dd=32), name="cast_l2q")
    scr_k = nc.dram_tensor("scr_l2k", [D, D], BF16, kind="Internal").ap()
    k.dma("gpsimd", scr_k.rearrange("(p hq dd) d -> p hq dd d", p=2, hq=16),
          wk_d.rearrange("(hq dd p) d -> p hq dd d", p=2, dd=32), name="cast_l2k")
    scr_v = wT_scratch(mk, "l2v", wv_d, D, D)
    scr_o = wT_scratch(mk, "l2o", wo_d, D, D)
    scr_1 = wT_scratch(mk, "l2f1", w1_d, H, D)
    scr_2 = wT_scratch(mk, "l2f2", w2_d, D, H)

    alloc_ln_scratch(mk)
    selrep = k.alloc("l2_sel", 128, 512, BF16)
    g4 = k.alloc("l2_g4", 128, 64, BF16)
    brd = k.alloc("l2_brd", 16, 1024, BF16)
    selv = k.alloc("l2_selv", 128, 256, BF16)
    bd64 = k.alloc("l2_bd", 128, 128, BF16)
    for buf, dd_ in ((selrep, sel_d), (g4, g4_d), (brd, brd_d), (selv, selv_d), (bd64, bd_d)):
        k.dma("sync", buf.ap(), dd_, name="ldc_" + buf.name)
    g1c = k.alloc("l2_g1c", 128, NCHUNK, F32)
    b1c = k.alloc("l2_b1c", 128, NCHUNK, F32)
    g2c = k.alloc("l2_g2c", 128, NCHUNK, F32)
    b2c = k.alloc("l2_b2c", 128, NCHUNK, F32)
    boc = k.alloc("l2_boc", 128, NCHUNK, F32)
    bf1c = k.alloc("l2_bf1c", 128, 32, F32)
    bf2c = k.alloc("l2_bf2c", 128, NCHUNK, F32)
    for row_d, dst, nch in ((g1_d, g1c, 8), (b1l_d, b1c, 8), (g2_d, g2c, 8),
                            (b2l_d, b2c, 8), (bo_d, boc, 8), (bf1_d, bf1c, 32),
                            (bf2_d, bf2c, 8)):
        mr = k.mark()
        rr = load_row(mk, "l2r_" + dst.name, row_d, nch * 128)
        col_chunks_from_row(mk, rr.ap(), dst, nch)
        k.release(mr)
    lamb = lam_broadcast(mk, "l2", mk.dram_in("L2_attn_lq1", [32]),
                         mk.dram_in("L2_attn_lk1", [32]),
                         mk.dram_in("L2_attn_lq2", [32]),
                         mk.dram_in("L2_attn_lk2", [32]), LAM_INIT, 32)
    neglam = k.alloc("l2_nlam", 128, 1, F32)
    k.ts(neglam.ap(), lamb.ap(), -1.0, None, ALU.mult, name="nlam")

    HT = TB // 2
    zb = k.alloc("l2_z", 128, NCHUNK * TB, BF16)
    outac = k.alloc("l2_out", 128, NCHUNK * TB, F32)
    wbuf = k.alloc("l2_w", 128, NCHUNK * D, BF16)

    for tb in range(NTB):
        t0 = tb * TB
        ln_tb(mk, tb, zb)

        def zfn(c, a, b_, t0=t0):
            return zb[:, c * TB + (a - t0):c * TB + (b_ - t0)]
        mA = k.mark()
        qT = k.alloc("l2_qT", 128, NCHUNK * TB, BF16)
        kTb = k.alloc("l2_kT", 128, NCHUNK * TB, BF16)
        vT = k.alloc("l2_vT", 128, NCHUNK * TB, BF16)
        mB = k.mark()
        xn = k.alloc("l2_xn", 128, NCHUNK * TB, BF16)
        ln_tb(mk, tb, xn, gamma_col=g1c, beta_col=b1c, src_fn=zfn, src_bf16=True)

        def xn_c(c):
            return xn[:, c * TB:(c + 1) * TB]
        for scr, dst in ((scr_q, qT), (scr_k, kTb), (scr_v, vT)):
            load_wT_chunks(mk, wbuf, scr, D, D, 0, NCHUNK, queue="sync")
            for co in range(NCHUNK):
                pb = k.banks[co % 2]
                for ci in range(NCHUNK):
                    k.mm(pb[:, 0:TB], wbuf[:, ci * D + co * 128:ci * D + co * 128 + 128],
                         xn_c(ci), start=(ci == 0), stop=(ci == NCHUNK - 1),
                         name=f"prj{co}")
                k.copy(dst[:, co * TB:(co + 1) * TB], pb[:, 0:TB], eng="vector",
                       name=f"prjev{co}")
        k.release(mB)
        elo = k.alloc("l2_elo", 16, 32 * HT, BF16)
        ehi = k.alloc("l2_ehi", 16, 32 * HT, BF16)
        dlo = k.alloc("l2_dlo", 16, HT, F32)
        dhi = k.alloc("l2_dhi", 16, HT, F32)
        krep = k.alloc("l2_krep", 128, HT, BF16)
        prod = k.alloc("l2_prod", 128, HT, BF16)
        a2 = k.alloc("l2_a2", 16, HT, F32)
        a2t = k.alloc("l2_a2t", 16, HT, F32)
        a2b = k.alloc("l2_a2b", 16, HT, BF16)
        vrep = k.alloc("l2_vrep", 128, HT, BF16)
        prod2 = k.alloc("l2_prod2", 128, HT, F32)
        osq = k.alloc("l2_osq", 128, TB, BF16)
        rstd2 = k.alloc("l2_rstd2", 128, TB, F32)
        abf = k.alloc("l2_abf", 128, NCHUNK * TB, BF16)
        for half in range(2):
            hs = half * HT
            for kp in range(32):
                kc = kp // 4
                pbk = k.banks[2]
                k.mm(pbk[:, 0:HT], selrep[:, (kp % 4) * 128:(kp % 4) * 128 + 128],
                     kTb[:, kc * TB + hs:kc * TB + hs + HT], start=True, stop=True,
                     name=f"krep{kp}")
                k.copy(krep.ap(), pbk[:, 0:HT], eng="vector", name=f"krepe{kp}")
                pl = k.banks[3]
                ph = k.banks[4]
                for c in range(NCHUNK):
                    k.tt(prod.ap(), qT[:, c * TB + hs:c * TB + hs + HT], krep.ap(),
                         ALU.mult, name=f"prod{kp}_{c}")
                    pb_t = pl if c < 4 else ph
                    cl = c % 4
                    k.mm(pb_t[:16, 0:HT], g4[:, cl * 16:(cl + 1) * 16], prod.ap(),
                         start=(cl == 0), stop=(cl == 3), name=f"gsum{kp}_{c}")
                k.act(elo[:, kp * HT:(kp + 1) * HT], pl[:16, 0:HT], AF.Exp,
                      bias=-20.0, scale=0.17677669529663687, name=f"explo{kp}")
                k.act(ehi[:, kp * HT:(kp + 1) * HT], ph[:16, 0:HT], AF.Exp,
                      bias=-20.0, scale=0.17677669529663687, name=f"exphi{kp}")
                if kp == 0:
                    k.copy(dlo.ap(), elo[:, 0:HT], eng="vector", name="d0lo")
                    k.copy(dhi.ap(), ehi[:, 0:HT], eng="vector", name="d0hi")
                else:
                    k.tt(dlo.ap(), dlo.ap(), elo[:, kp * HT:(kp + 1) * HT], ALU.add,
                         name=f"dlo{kp}")
                    k.tt(dhi.ap(), dhi.ap(), ehi[:, kp * HT:(kp + 1) * HT], ALU.add,
                         name=f"dhi{kp}")
            k.recip(dlo.ap(), dlo.ap(), name="rdlo")
            k.recip(dhi.ap(), dhi.ap(), name="rdhi")
            for hk in range(16):
                k.tt(a2.ap(), elo[:, hk * HT:(hk + 1) * HT], dlo.ap(), ALU.mult,
                     name=f"a2m1_{hk}")
                k.tt(a2t.ap(), ehi[:, (16 + hk) * HT:(17 + hk) * HT], dhi.ap(), ALU.mult,
                     name=f"a2m2_{hk}")
                k.stt(a2b.ap(), a2t.ap(), neglam[0:16, 0:1], a2.ap(), ALU.mult, ALU.add,
                      name=f"a2c_{hk}")
                pbv = k.banks[5]
                k.mm(pbv[:, 0:HT], selv[:, (hk % 2) * 128:(hk % 2) * 128 + 128],
                     vT[:, (hk // 2) * TB + hs:(hk // 2) * TB + hs + HT],
                     start=True, stop=True, name=f"vrep{hk}")
                k.copy(vrep.ap(), pbv[:, 0:HT], eng="vector", name=f"vrepe{hk}")
                if tb == NTB - 1 and half == 1 and hk == 15:
                    mk.dump("l2_a2b", a2b)
                    mk.dump("l2_vrep", vrep)
                    mk.dump("l2_lamb", lamb)
                for c in range(NCHUNK):
                    pba = k.banks[6 + (c % 2)]
                    k.mm(pba[:, 0:HT], brd[:, c * 128:(c + 1) * 128], a2b.ap(),
                         start=True, stop=True, name=f"abr{hk}_{c}")
                    dst = outac[:, c * TB + hs:c * TB + hs + HT]
                    if hk == 0:
                        k.tt(dst, pba[:, 0:HT], vrep.ap(), ALU.mult, name=f"oa{hk}_{c}")
                    else:
                        k.tt(prod2.ap(), pba[:, 0:HT], vrep.ap(), ALU.mult,
                             name=f"oam{hk}_{c}")
                        k.tt(dst, dst, prod2.ap(), ALU.add, name=f"oaa{hk}_{c}")
        if tb == NTB - 1:
            for _dn, _db in (("l2_elo", elo), ("l2_ehi", ehi), ("l2_dlo", dlo),
                             ("l2_dhi", dhi), ("l2_out", outac), ("l2_vT", vT),
                             ("l2_qT", qT), ("l2_kTb", kTb)):
                mk.dump(_dn, _db)
        for c in range(NCHUNK):
            k.copy(osq.ap(), outac[:, c * TB:(c + 1) * TB], eng="vector", name=f"sqc{c}")
            k.tt(osq.ap(), osq.ap(), osq.ap(), ALU.mult, name=f"sq{c}")
            pbs = k.banks[2]
            k.mm(pbs[:, 0:TB], bd64.ap(), osq.ap(), start=True, stop=True, name=f"ms{c}")
            k.ts(rstd2.ap(), pbs[:, 0:TB], 1.0 / 64, 1e-5, ALU.mult, ALU.add,
                 name=f"msc{c}")
            k.act(rstd2.ap(), rstd2.ap(), AF.Sqrt, name=f"mssq{c}")
            k.recip(rstd2.ap(), rstd2.ap(), name=f"msr{c}")
            k.ts(rstd2.ap(), rstd2.ap(), 1.0 - LAM_INIT, None, ALU.mult, name=f"msl{c}")
            k.tt(abf[:, c * TB:(c + 1) * TB], outac[:, c * TB:(c + 1) * TB],
                 rstd2.ap(), ALU.mult, name=f"abf{c}")
        load_wT_chunks(mk, wbuf, scr_o, D, D, 0, NCHUNK, queue="sync")
        for co in range(NCHUNK):
            pb = k.banks[co % 2]
            for ci in range(NCHUNK):
                k.mm(pb[:, 0:TB], wbuf[:, ci * D + co * 128:ci * D + co * 128 + 128],
                     abf[:, ci * TB:(ci + 1) * TB], start=(ci == 0),
                     stop=(ci == NCHUNK - 1), name=f"wo{co}")
            k.ts(outac[:, co * TB:(co + 1) * TB], pb[:, 0:TB], boc[:, co:co + 1],
                 None, ALU.add, name=f"boadd{co}")
            k.tt(zb[:, co * TB:(co + 1) * TB], zb[:, co * TB:(co + 1) * TB],
                 outac[:, co * TB:(co + 1) * TB], ALU.add, name=f"conn{co}")
        k.release(mA)
        xn = k.alloc("l2_xn2", 128, NCHUNK * TB, BF16)
        hg = k.alloc("l2_hg", 128, 8 * TB, BF16)
        w1b = k.alloc("l2_w1b", 128, 8 * 1024, BF16)
        w2b = k.alloc("l2_w2b", 128, 8 * 1024, BF16)

        def xn_c2(c):
            return xn[:, c * TB:(c + 1) * TB]
        ln_tb(mk, tb, xn, gamma_col=g2c, beta_col=b2c, src_fn=zfn, src_bf16=True)
        for hb in range(4):
            for ci in range(NCHUNK):
                k.dma("sync", w1b[:, ci * 1024:(ci + 1) * 1024],
                      scr_1[hb * 1024:(hb + 1) * 1024, ci * 128:(ci + 1) * 128],
                      transpose=True, name=f"f1T{hb}_{ci}")
                k.dma("sync", w2b[:, ci * 1024:(ci + 1) * 1024],
                      scr_2[:, (hb * 8 + ci) * 128:(hb * 8 + ci + 1) * 128],
                      transpose=True, name=f"f2T{hb}_{ci}")
            for co in range(8):
                ch = hb * 8 + co
                pbh = k.banks[co % 2]
                for ci in range(NCHUNK):
                    k.mm(pbh[:, 0:TB], w1b[:, ci * 1024 + co * 128:ci * 1024 + co * 128 + 128],
                         xn_c2(ci), start=(ci == 0), stop=(ci == NCHUNK - 1),
                         name=f"fw1{ch}")
                k.act(hg[:, co * TB:(co + 1) * TB], pbh[:, 0:TB], AF.Relu,
                      bias=bf1c[:, ch:ch + 1], name=f"relu{ch}")
            for co in range(NCHUNK):
                pbo = k.banks[2 + co % 2]
                for ci in range(NCHUNK):
                    k.mm(pbo[:, 0:TB], w2b[:, ci * 1024 + co * 128:ci * 1024 + co * 128 + 128],
                         hg[:, ci * TB:(ci + 1) * TB], start=(ci == 0),
                         stop=(ci == NCHUNK - 1), name=f"fw2{hb}_{co}")
                dst = outac[:, co * TB:(co + 1) * TB]
                if hb == 0:
                    k.copy(dst, pbo[:, 0:TB], eng="vector", name=f"fo{hb}_{co}")
                else:
                    k.tt(dst, dst, pbo[:, 0:TB], ALU.add, name=f"fo{hb}_{co}")
        for co in range(NCHUNK):
            k.ts(outac[:, co * TB:(co + 1) * TB], outac[:, co * TB:(co + 1) * TB],
                 bf2c[:, co:co + 1], None, ALU.add, name=f"fb2{co}")
            dst = mk.rT_ap(co, t0, t0 + TB)
            k.tt(dst, dst, zb[:, co * TB:(co + 1) * TB], ALU.add, name=f"cadd{co}")
            k.tt(dst, dst, outac[:, co * TB:(co + 1) * TB], ALU.add, name=f"fradd{co}")
        k.release(mA)


MODULE_EMITTERS["L2_attn"] = emit_L2_attn


def emit_quant(mk, tile, N, qs):
    k = mk.k
    G = N // 16
    t3 = tile.rearrange("p (g i) -> p g i", i=16)
    amax, sc = qs["amax"], qs["sc"]
    vb, rs, sxb, acc, tmp = qs["vb"], qs["rs"], qs["sxb"], qs["acc"], qs["tmp"]
    k.red(amax[:, 0:G], tile.rearrange("p (g i) -> p g i", i=16), ALU.max, absval=True,
          name="qamax")
    k.ts(sc[:, 0:G], amax[:, 0:G], 1.0 / 6.0, 1e-12, ALU.mult, ALU.max, name="qsc")
    k.recip(amax[:, 0:G], sc[:, 0:G], name="qrc")
    rs3 = rs.ap()[:, 0:N].rearrange("p (g i) -> p g i", i=16)
    for i in range(16):
        k.copy(rs3[:, :, i], amax[:, 0:G], eng="vector", name=f"qre{i}")
    k.act(vb.ap()[:, 0:N], tile, AF.Abs, name="qabs")
    k.act(sxb.ap()[:, 0:N], tile, AF.Sign, name="qsign")
    k.tt(vb.ap()[:, 0:N], vb.ap()[:, 0:N], rs.ap()[:, 0:N], ALU.mult, name="qv")
    k.ts(acc.ap()[:, 0:N], vb.ap()[:, 0:N], 0.25, 0.5, ALU.is_gt, ALU.mult, name="qt0")
    for th, w in ((0.75, 0.5), (1.25, 0.5), (1.75, 0.5), (2.5, 1.0), (3.5, 1.0), (5.0, 2.0)):
        k.ts(tmp.ap()[:, 0:N], vb.ap()[:, 0:N], th, w, ALU.is_gt, ALU.mult,
             name=f"qt{th}")
        k.tt(acc.ap()[:, 0:N], acc.ap()[:, 0:N], tmp.ap()[:, 0:N], ALU.add,
             name=f"qa{th}")
    for i in range(16):
        k.copy(rs3[:, :, i], sc[:, 0:G], eng="vector", name=f"qse{i}")
    k.tt(acc.ap()[:, 0:N], acc.ap()[:, 0:N], sxb.ap()[:, 0:N], ALU.mult, name="qsgn")
    k.tt(tile, acc.ap()[:, 0:N], rs.ap()[:, 0:N], ALU.mult, name="qout")


def alloc_quant_scratch(mk, N):
    k = mk.k
    G = N // 16
    return {
        "amax": k.alloc("q_amax", 128, G, F32),
        "sc": k.alloc("q_sc", 128, G, F32),
        "vb": k.alloc("q_vb", 128, N, BF16),
        "rs": k.alloc("q_rs", 128, N, BF16),
        "sxb": k.alloc("q_sxb", 128, N, BF16),
        "acc": k.alloc("q_acc", 128, N, BF16),
        "tmp": k.alloc("q_tmp", 128, N, BF16),
    }


def emit_L3_ffn(mk):
    k, nc = mk.k, mk.nc
    I = 2816
    NI = I // 128
    wg_d = mk.dram_in("L3_ffn_w_gate", [I, D])
    wu_d = mk.dram_in("L3_ffn_w_up", [I, D])
    wd_d = mk.dram_in("L3_ffn_w_down", [D, I])
    qg_scr = nc.dram_tensor("scr_l3fg", [I, D], BF16, kind="Internal").ap()
    qu_scr = nc.dram_tensor("scr_l3fu", [I, D], BF16, kind="Internal").ap()
    qd_scr = nc.dram_tensor("scr_l3fd", [D, I], BF16, kind="Internal").ap()

    alloc_ln_scratch(mk)
    qs = alloc_quant_scratch(mk, I)
    mq = k.mark()
    wrow = k.alloc("l3f_wrow", 128, I, BF16)
    for src_d, scr, nch, ncols in ((wg_d, qg_scr, NI, D), (wu_d, qu_scr, NI, D),
                                   (wd_d, qd_scr, NCHUNK, I)):
        for ch in range(nch):
            k.dma("gpsimd", wrow[:, 0:ncols], src_d[ch * 128:(ch + 1) * 128, :],
                  name=f"qw_ld{ch}")
            emit_quant(mk, wrow[:, 0:ncols], ncols, qs)
            k.dma("gpsimd", scr[ch * 128:(ch + 1) * 128, :], wrow[:, 0:ncols],
                  name=f"qw_st{ch}")
    k.release(mq)

    xn = k.alloc("l3f_xn", 128, NCHUNK * TB, BF16)
    xqT = k.alloc("l3f_xq", 128, NCHUNK * TB, BF16)
    xtok = k.alloc("l3f_xtok", 128, D, BF16)
    wT = k.alloc("l3f_wT", 128, NCHUNK * I, BF16)
    gbuf = k.alloc("l3f_g", 128, NI * TB, BF16)
    htok = k.alloc("l3f_htok", 128, I, BF16)

    for tb in range(NTB):
        t0 = tb * TB
        ln_tb(mk, tb, xn)
        for st_ in range(4):
            for c in range(NCHUNK):
                pt = k.banks_bf[c % 2]
                k.trans(pt[0:128, 0:128], xn[:, c * TB + st_ * 128:c * TB + st_ * 128 + 128],
                        mk.idb.ap(), name=f"xt{st_}_{c}")
                k.copy(xtok[:, c * 128:(c + 1) * 128], pt[0:128, 0:128], eng="vector",
                       name=f"xte{st_}_{c}")
            emit_quant(mk, xtok.ap(), D, qs)
            for c in range(NCHUNK):
                pt = k.banks_bf[c % 2]
                k.trans(pt[0:128, 0:128], xtok[:, c * 128:(c + 1) * 128], mk.idb.ap(),
                        name=f"xb{st_}_{c}")
                k.copy(xqT[:, c * TB + st_ * 128:c * TB + st_ * 128 + 128],
                       pt[0:128, 0:128], eng="vector", name=f"xbe{st_}_{c}")
        for ci in range(NCHUNK):
            k.dma("sync", wT[:, ci * I:(ci + 1) * I], qg_scr[:, ci * 128:(ci + 1) * 128],
                  transpose=True, name=f"gT{ci}")
        for co in range(NI):
            pb = k.banks[2 + co % 2]
            for ci in range(NCHUNK):
                k.mm(pb[:, 0:TB], wT[:, ci * I + co * 128:ci * I + co * 128 + 128],
                     xqT[:, ci * TB:(ci + 1) * TB], start=(ci == 0),
                     stop=(ci == NCHUNK - 1), name=f"gmm{co}")
            k.act(gbuf[:, co * TB:(co + 1) * TB], pb[:, 0:TB], AF.Silu, name=f"silu{co}")
        for ci in range(NCHUNK):
            k.dma("sync", wT[:, ci * I:(ci + 1) * I], qu_scr[:, ci * 128:(ci + 1) * 128],
                  transpose=True, name=f"uT{ci}")
        for co in range(NI):
            pb = k.banks[2 + co % 2]
            for ci in range(NCHUNK):
                k.mm(pb[:, 0:TB], wT[:, ci * I + co * 128:ci * I + co * 128 + 128],
                     xqT[:, ci * TB:(ci + 1) * TB], start=(ci == 0),
                     stop=(ci == NCHUNK - 1), name=f"umm{co}")
            k.tt(gbuf[:, co * TB:(co + 1) * TB], gbuf[:, co * TB:(co + 1) * TB],
                 pb[:, 0:TB], ALU.mult, name=f"hmul{co}")
        for st_ in range(4):
            for c in range(NI):
                pt = k.banks_bf[c % 2]
                k.trans(pt[0:128, 0:128], gbuf[:, c * TB + st_ * 128:c * TB + st_ * 128 + 128],
                        mk.idb.ap(), name=f"ht{st_}_{c}")
                k.copy(htok[:, c * 128:(c + 1) * 128], pt[0:128, 0:128], eng="vector",
                       name=f"hte{st_}_{c}")
            emit_quant(mk, htok.ap(), I, qs)
            for c in range(NI):
                pt = k.banks_bf[c % 2]
                k.trans(pt[0:128, 0:128], htok[:, c * 128:(c + 1) * 128], mk.idb.ap(),
                        name=f"hb{st_}_{c}")
                k.copy(gbuf[:, c * TB + st_ * 128:c * TB + st_ * 128 + 128],
                       pt[0:128, 0:128], eng="vector", name=f"hbe{st_}_{c}")
        for ci in range(NI):
            k.dma("sync", wT[:, ci * 1024:(ci + 1) * 1024],
                  qd_scr[:, ci * 128:(ci + 1) * 128], transpose=True, name=f"dT{ci}")
        for co in range(NCHUNK):
            pb = k.banks[2 + co % 2]
            for ci in range(NI):
                k.mm(pb[:, 0:TB], wT[:, ci * 1024 + co * 128:ci * 1024 + co * 128 + 128],
                     gbuf[:, ci * TB:(ci + 1) * TB], start=(ci == 0),
                     stop=(ci == NI - 1), name=f"dmm{co}")
            dst = mk.rT_ap(co, t0, t0 + TB)
            k.tt(dst, dst, pb[:, 0:TB], ALU.add, name=f"dradd{co}")


MODULE_EMITTERS["L3_ffn"] = emit_L3_ffn


def emit_L2_ffn(mk):
    k, nc = mk.k, mk.nc
    E = 16
    sc_d = mk.dram_in("L2_ffn_scale", [D])
    gw_d = mk.dram_in("L2_ffn_gate_w", [E, D])
    gb_d = mk.dram_in("L2_ffn_gate_b", [E])
    m1w_d = mk.dram_in("L2_ffn_mlp1_w", [E, 2048, D])
    m1b_d = mk.dram_in("L2_ffn_mlp1_b", [E, 2048])
    m2w_d = mk.dram_in("L2_ffn_mlp2_w", [E, D, D])
    m2b_d = mk.dram_in("L2_ffn_mlp2_b", [E, D])

    scr1 = nc.dram_tensor("scr_moe1", [E * D, D], BF16, kind="Internal").ap()
    for e in range(E):
        k.dma("gpsimd", scr1[e * D:(e + 1) * D, :],
              m1w_d[e, 0:2048:2, :], name=f"c1_{e}")
    scr2 = nc.dram_tensor("scr_moe2", [E * D, D], BF16, kind="Internal").ap()
    k.dma("gpsimd", scr2.rearrange("(e h) d -> e h d", e=E),
          m2w_d, name="c2_moe")
    scrg = nc.dram_tensor("scr_moeg", [E, D], BF16, kind="Internal").ap()
    k.dma("gpsimd", scrg, gw_d, name="cg_moe")

    alloc_ln_scratch(mk)
    scc = k.alloc("moe_scc", 128, NCHUNK, F32)
    b1ec = k.alloc("moe_b1ec", 128, NCHUNK * E, F32)
    gbc = k.alloc("moe_gbc", 16, 1, F32)
    b2all = k.alloc("moe_b2", 16, D, BF16)
    mr = k.mark()
    rr = load_row(mk, "moe_scr", sc_d, D)
    col_chunks_from_row(mk, rr.ap(), scc, NCHUNK)
    grow = load_row(mk, "moe_gbr", gb_d, E)
    pb = k.banks[7]
    k.mm(pb[:16, 0:1], grow[0:1, 0:16], k.const_col(1.0)[0:1, 0:1], start=True,
         stop=True, name="gbT")
    k.copy(gbc.ap(), pb[:16, 0:1], eng="vector", name="gbTe")
    for e in range(E):
        mrb = k.mark()
        b1r = load_row(mk, f"moe_b1r{e}", m1b_d[e, :], 2048)
        for ch in range(NCHUNK):
            pb = k.banks[6]
            k.mm(pb[:, 0:1], b1r[0:1, ch * 256:ch * 256 + 256:2],
                 k.const_col(1.0)[0:1, 0:1], start=True, stop=True, name=f"b1T{e}_{ch}")
            k.copy(b1ec[:, e * NCHUNK + ch:e * NCHUNK + ch + 1], pb[:, 0:1],
                   eng="vector", name=f"b1Te{e}_{ch}")
        k.release(mrb)
    k.release(mr)
    k.dma("gpsimd", b2all.ap(), m2b_d, name="ld_b2all")

    zb = k.alloc("moe_z", 128, NCHUNK * TB, BF16)
    tbuf = k.alloc("moe_t", 128, NCHUNK * TB, BF16)
    gsb = k.alloc("moe_gsb", 16, TB, BF16)
    gtok = k.alloc("moe_gtok", 128, 16, F32)
    work = k.alloc("moe_work", 128, 16, F32)
    eqm = k.alloc("moe_eqm", 128, 16, F32)
    mcol = k.alloc("moe_mcol", 128, 4, F32)
    route = k.alloc("moe_route", 16, TB, BF16)
    wtok = k.alloc("moe_wtok", 128, 4 * 16, F32)
    zsq = k.alloc("moe_zsq", 128, TB, BF16)
    rstd = k.alloc("moe_rstd", 128, TB, F32)
    gwT = k.alloc("moe_gwT", 128, NCHUNK * 16, BF16)
    w1T = k.alloc("moe_w1T", 128, NCHUNK * D, BF16)
    w2T = k.alloc("moe_w2T", 128, NCHUNK * D, BF16)
    hbuf = k.alloc("moe_h", 128, NCHUNK * TB, BF16)
    sgb = k.alloc("moe_sg", 128, TB, BF16)
    acctok = k.alloc("moe_acc", 128, 4 * D, F32)
    wtb = k.alloc("moe_wtb", 128, 16, BF16)

    for ci in range(NCHUNK):
        k.dma("sync", gwT[:, ci * 16:(ci + 1) * 16], scrg[:, ci * 128:(ci + 1) * 128],
              transpose=True, name=f"gwT{ci}")

    for tb in range(NTB):
        t0 = tb * TB
        ln_tb(mk, tb, zb)
        ps_sq = k.banks[6]
        for c in range(NCHUNK):
            zc = zb[:, c * TB:(c + 1) * TB]
            k.act(zsq.ap(), zc, AF.Square, name=f"msq{c}")
            k.mm(ps_sq[:, 0:TB], mk.oneb.ap(), zsq.ap(), start=(c == 0),
                 stop=(c == NCHUNK - 1), name=f"msum{c}")
        k.ts(rstd.ap(), ps_sq[:, 0:TB], 1.0 / D, 1e-5, ALU.mult, ALU.add, name="mms")
        k.act(rstd.ap(), rstd.ap(), AF.Sqrt, name="msqrt")
        k.recip(rstd.ap(), rstd.ap(), name="mrec")
        for c in range(NCHUNK):
            k.tt(tbuf[:, c * TB:(c + 1) * TB], zb[:, c * TB:(c + 1) * TB], rstd.ap(),
                 ALU.mult, name=f"tmul{c}")
            k.ts(tbuf[:, c * TB:(c + 1) * TB], tbuf[:, c * TB:(c + 1) * TB],
                 scc[:, c:c + 1], None, ALU.mult, name=f"tsc{c}")
        pg = k.banks[5]
        for ci in range(NCHUNK):
            k.mm(pg[:16, 0:TB], gwT[:, ci * 16:(ci + 1) * 16],
                 tbuf[:, ci * TB:(ci + 1) * TB], start=(ci == 0),
                 stop=(ci == NCHUNK - 1), name=f"gmm{ci}")
        k.ts(gsb.ap(), pg[:16, 0:TB], gbc[0:16, 0:1], None, ALU.add, name="gadd")
        for st_ in range(4):
            pt = k.banks_bf[4]
            k.trans(pt[0:128, 0:16], gsb[:, st_ * 128:(st_ + 1) * 128],
                    mk.idb.ap()[0:16, 0:16], name=f"gt{st_}")
            k.copy(gtok.ap(), pt[0:128, 0:16], eng="vector", name=f"gte{st_}")
            k.copy(work.ap(), gtok.ap(), eng="vector", name=f"wk{st_}")
            for r_ in range(4):
                k.red(mcol[:, r_:r_ + 1], work.ap(), ALU.max, name=f"mx{st_}_{r_}")
                if r_ < 3:
                    k.ts(eqm.ap(), work.ap(), mcol[:, r_:r_ + 1], -1e30,
                         ALU.is_ge, ALU.mult, name=f"eq{st_}_{r_}")
                    k.tt(work.ap(), work.ap(), eqm.ap(), ALU.add, name=f"wm{st_}_{r_}")
            k.ts(eqm.ap(), gtok.ap(), mcol[:, 3:4], None, ALU.is_ge, name=f"msk{st_}")
            k.ts(work[:, 0:1], mcol[:, 0:1], -1.0, None, ALU.mult, name=f"negm{st_}")
            k.act(gtok.ap(), gtok.ap(), AF.Exp, bias=work[:, 0:1], name=f"gexp{st_}")
            k.tt(gtok.ap(), gtok.ap(), eqm.ap(), ALU.mult, name=f"gmask{st_}")
            k.red(work[:, 0:1], gtok.ap(), ALU.add, name=f"gsum{st_}")
            k.recip(work[:, 0:1], work[:, 0:1], name=f"grec{st_}")
            k.ts(wtok[:, st_ * 16:(st_ + 1) * 16], gtok.ap(), work[:, 0:1], None,
                 ALU.mult, name=f"gwts{st_}")
            pt2 = k.banks_bf[3]
            k.copy(wtb.ap(), wtok[:, st_ * 16:(st_ + 1) * 16], eng="vector",
                   name=f"wtb{st_}")
            k.trans(pt2[0:16, 0:128], wtb.ap(), mk.idb.ap(), name=f"rt{st_}")
            k.copy(route[:, st_ * 128:(st_ + 1) * 128], pt2[0:16, 0:128],
                   eng="vector", name=f"rte{st_}")
        for e in range(E):
            for ci in range(NCHUNK):
                k.dma("sync", w1T[:, ci * D:(ci + 1) * D],
                      scr1[e * D:(e + 1) * D, ci * 128:(ci + 1) * 128],
                      transpose=True, name=f"w1T{e}_{ci}")
                k.dma("sync", w2T[:, ci * D:(ci + 1) * D],
                      scr2[e * D:(e + 1) * D, ci * 128:(ci + 1) * 128],
                      transpose=True, name=f"w2T{e}_{ci}")
            for co in range(NCHUNK):
                pb = k.banks[co % 2]
                for ci in range(NCHUNK):
                    k.mm(pb[:, 0:TB], w1T[:, ci * D + co * 128:ci * D + co * 128 + 128],
                         tbuf[:, ci * TB:(ci + 1) * TB], start=(ci == 0),
                         stop=(ci == NCHUNK - 1), name=f"e1mm{e}_{co}")
                k.ts(hbuf[:, co * TB:(co + 1) * TB], pb[:, 0:TB],
                     b1ec[:, e * NCHUNK + co:e * NCHUNK + co + 1], 7.0,
                     ALU.add, ALU.min, name=f"xglu{e}_{co}")
                k.act(sgb.ap(), hbuf[:, co * TB:(co + 1) * TB], AF.Sigmoid,
                      scale=1.702, name=f"sig{e}_{co}")
                k.stt(hbuf[:, co * TB:(co + 1) * TB], hbuf[:, co * TB:(co + 1) * TB],
                      8.0, sgb.ap(), ALU.mult, ALU.mult, name=f"sw{e}_{co}")
            for st_ in range(4):
                for fh in range(2):
                    pb = k.banks[2 + fh]
                    for ci in range(NCHUNK):
                        k.mm(pb[:, 0:TB],
                             hbuf[:, ci * TB + st_ * 128:ci * TB + st_ * 128 + 128],
                             w2T[:, ci * D + fh * TB:ci * D + fh * TB + TB],
                             start=(ci == 0), stop=(ci == NCHUNK - 1),
                             name=f"e2mm{e}_{st_}_{fh}")
                    dst = acctok[:, st_ * D + fh * TB:st_ * D + fh * TB + TB]
                    if e == 0:
                        k.ts(dst, pb[:, 0:TB], wtok[:, st_ * 16 + e:st_ * 16 + e + 1],
                             None, ALU.mult, name=f"acc0{e}_{st_}_{fh}")
                    else:
                        k.stt(dst, pb[:, 0:TB], wtok[:, st_ * 16 + e:st_ * 16 + e + 1],
                              dst, ALU.mult, ALU.add, name=f"acc{e}_{st_}_{fh}")
        for st_ in range(4):
            for fh in range(2):
                pb = k.banks[4]
                k.mm(pb[:, 0:TB], route[:, st_ * 128:(st_ + 1) * 128],
                     b2all[:, fh * TB:fh * TB + TB], start=True, stop=True,
                     name=f"b2mm{st_}_{fh}")
                dst = acctok[:, st_ * D + fh * TB:st_ * D + fh * TB + TB]
                k.tt(dst, dst, pb[:, 0:TB], ALU.add, name=f"b2a{st_}_{fh}")
        for co in range(NCHUNK):
            dst = mk.rT_ap(co, t0, t0 + TB)
            k.tt(dst, dst, zb[:, co * TB:(co + 1) * TB], ALU.add, name=f"zadd{co}")
        for st_ in range(4):
            for co in range(NCHUNK):
                pt = k.banks[6 + st_ % 2]
                k.trans(pt[0:128, 0:128], acctok[:, st_ * D + co * 128:st_ * D + co * 128 + 128],
                        mk.idf.ap(), name=f"atr{st_}_{co}")
                dst = mk.rT_ap(co, t0 + st_ * 128, t0 + (st_ + 1) * 128)
                k.tt(dst, dst, pt[0:128, 0:128], ALU.add, name=f"aadd{st_}_{co}")


MODULE_EMITTERS["L2_ffn"] = emit_L2_ffn


def emit_L1_attn(mk):
    k, nc = mk.k, mk.nc
    import math
    LAM_INIT = 0.8 - 0.6 * math.exp(-0.3 * 12)
    wq_d = mk.dram_in("L1_attn_Wq", [D, D])
    wk_d = mk.dram_in("L1_attn_Wk", [D, D])
    wv_d = mk.dram_in("L1_attn_Wv", [D, D])
    wo_d = mk.dram_in("L1_attn_Wo", [D, D])
    rmsw_d = mk.dram_in("L1_attn_rms_w", [64])
    cm_d = mk.dram_in("l1_causmask", [128, 512], BF16)

    alloc_ln_scratch(mk)
    lamb = lam_broadcast(mk, "l1a", mk.dram_in("L1_attn_lq1", [32]),
                         mk.dram_in("L1_attn_lk1", [32]),
                         mk.dram_in("L1_attn_lq2", [32]),
                         mk.dram_in("L1_attn_lk2", [32]), LAM_INIT, 32)
    neglam = k.alloc("l1_nlam", 128, 1, F32)
    k.ts(neglam.ap(), lamb.ap(), -1.0, None, ALU.mult, name="nlam1")
    cmask = k.alloc("l1_cm", 128, 512, BF16)
    k.dma("sync", cmask.ap(), cm_d, name="ld_cm1")
    rwb = k.alloc("l1_rwb", 128, 64, F32)
    mr = k.mark()
    rr = load_row(mk, "l1_rw", rmsw_d, 64)
    onesr1 = k.alloc("l1_or", 1, 128, F32)
    k.memset(onesr1.ap(), 1.0)
    pb = k.banks[7]
    k.mm(pb[:, 0:64], onesr1.ap(), rr[0:1, 0:64], start=True, stop=True,
         name="rwbc")
    k.ts(rwb.ap(), pb[:, 0:64], 1.0 - LAM_INIT, None, ALU.mult, name="rwsc")
    k.release(mr)

    xn = k.alloc("l1_xn", 128, NCHUNK * S, BF16)
    aT = k.alloc("l1_aT", 128, NCHUNK * S, BF16)
    for tb in range(NTB):
        ln_tb(mk, tb, xn, dst_cols_per_chunk=S, dst_t0=tb * TB)
    mA = k.mark()
    wsl = k.alloc("l1_wsl", 128, 3 * NCHUNK * 64, BF16)
    qpr = k.alloc("l1_qp", 64, S, BF16)
    kpr = k.alloc("l1_kp", 64, S, BF16)
    vtk = k.alloc("l1_vt", 128, NTILE * 65, BF16)
    ebuf = [k.alloc(f"l1_e{j}", 128, NTILE * 256, BF16) for j in range(2)]
    rec = k.alloc("l1_rec", 128, 1, F32)
    att0 = k.alloc("l1_a0", 128, 64, F32)
    att1 = k.alloc("l1_a1", 128, 64, F32)
    msc = k.alloc("l1_msc", 128, 1, F32)
    abf = k.alloc("l1_abf", 128, 64, BF16)

    for t in range(NTILE):
        k.memset(vtk[:, t * 65 + 64:t * 65 + 65], 1.0, name=f"v1s{t}")
    for h in range(16):
        for wi, wd in enumerate((wq_d, wk_d, wv_d)):
            out3 = wsl.ap()[:, wi * NCHUNK * 64:(wi + 1) * NCHUNK * 64].rearrange(
                "p (c n) -> p c n", n=64)
            in3 = wd.rearrange("(c p) n -> p c n", p=128)[:, :, 64 * h:64 * h + 64]
            k.dma("gpsimd", out3, in3, name=f"wsl{h}_{wi}")

        def wslc(wi, ci):
            return wsl[:, wi * NCHUNK * 64 + ci * 64:wi * NCHUNK * 64 + (ci + 1) * 64]
        for tb in range(NTB):
            t0 = tb * TB
            for wi, dstb in ((0, qpr), (1, kpr)):
                pbq = k.banks[wi]
                for ci in range(NCHUNK):
                    k.mm(pbq[:64, 0:TB], wslc(wi, ci),
                         xn[:, ci * S + t0:ci * S + t0 + TB],
                         start=(ci == 0), stop=(ci == NCHUNK - 1), name=f"qp{h}_{tb}")
                k.copy(dstb[:, t0:t0 + TB], pbq[:64, 0:TB], eng="vector",
                       name=f"qpe{h}_{tb}")
        for t in range(NTILE):
            pbv = k.banks[2]
            for ci in range(NCHUNK):
                k.mm(pbv[:, 0:64], xn[:, ci * S + t * 128:ci * S + (t + 1) * 128],
                     wslc(2, ci), start=(ci == 0), stop=(ci == NCHUNK - 1),
                     name=f"vp{h}_{t}")
            k.copy(vtk[:, t * 65:t * 65 + 64], pbv[:, 0:64], eng="vector",
                   name=f"vpe{h}_{t}")
        for qb in range(8):
            q0 = qb * 256
            nkt = 2 * qb + 2
            po = [k.banks[3], k.banks[4], k.banks[5], k.banks[6]]
            for kt in range(nkt):
                pss = k.banks[7 - (kt % 1) * 0]
                for j in range(2):
                    k.mm(pss[:, 0:256], kpr[32 * j:32 * j + 32, kt * 128:(kt + 1) * 128],
                         qpr[32 * j:32 * j + 32, q0:q0 + 256],
                         start=True, stop=True, name=f"st{h}_{qb}_{kt}_{j}")
                    if kt >= nkt - 2:
                        dk = kt * 128 - q0
                        k.tt(pss[:, 0:256], pss[:, 0:256],
                             cmask[:, (dk // 128) * 256:(dk // 128) * 256 + 256],
                             ALU.add, name=f"cm{h}_{qb}_{kt}_{j}")
                    edst = ebuf[j][:, kt * 256:(kt + 1) * 256]
                    k.act(edst, pss[:, 0:256], AF.Exp, bias=-10.0,
                          scale=0.17677669529663687, name=f"ex{h}_{qb}_{kt}_{j}")
                    for i in range(2):
                        k.mm(po[2 * j + i][:, 0:65],
                             ebuf[j][:, kt * 256 + i * 128:kt * 256 + (i + 1) * 128],
                             vtk[:, kt * 65:(kt + 1) * 65],
                             start=(kt == 0), stop=(kt == nkt - 1),
                             name=f"pv{h}_{qb}_{kt}_{j}_{i}")
            for i in range(2):
                k.recip(rec.ap(), po[i][:, 64:65], name=f"r0{h}_{qb}_{i}")
                k.ts(att0.ap(), po[i][:, 0:64], rec[:, 0:1], None, ALU.mult,
                     name=f"n0{h}_{qb}_{i}")
                k.recip(rec.ap(), po[2 + i][:, 64:65], name=f"r1{h}_{qb}_{i}")
                k.ts(att1.ap(), po[2 + i][:, 0:64], rec[:, 0:1], neglam[:, 0:1],
                     ALU.mult, ALU.mult, name=f"n1{h}_{qb}_{i}")
                k.tt(att0.ap(), att0.ap(), att1.ap(), ALU.add, name=f"cmb{h}_{qb}_{i}")
                k.tt(att1.ap(), att0.ap(), att0.ap(), ALU.mult, name=f"sq{h}_{qb}_{i}")
                k.red(msc.ap(), att1.ap(), ALU.add, name=f"ms{h}_{qb}_{i}")
                k.ts(msc.ap(), msc.ap(), 1.0 / 64, 1e-5, ALU.mult, ALU.add,
                     name=f"msa{h}_{qb}_{i}")
                k.act(msc.ap(), msc.ap(), AF.Sqrt, name=f"mss{h}_{qb}_{i}")
                k.recip(msc.ap(), msc.ap(), name=f"msr{h}_{qb}_{i}")
                k.ts(att0.ap(), att0.ap(), msc[:, 0:1], None, ALU.mult,
                     name=f"msm{h}_{qb}_{i}")
                k.tt(abf.ap(), att0.ap(), rwb.ap(), ALU.mult, name=f"rw{h}_{qb}_{i}")
                ptr = k.banks_bf[7]
                k.trans(ptr[0:64, 0:128], abf.ap(), mk.idb.ap(), name=f"tr{h}_{qb}_{i}")
                dst = aT[64 * (h % 2):64 * (h % 2) + 64,
                         (h // 2) * S + q0 + i * 128:(h // 2) * S + q0 + (i + 1) * 128]
                k.copy(dst, ptr[0:64, 0:128], eng="vector", name=f"ae{h}_{qb}_{i}")
    k.release(mA)
    won = load_wN(mk, "l1_won", wo_d, D, D)
    for tb in range(NTB):
        t0 = tb * TB
        for co in range(NCHUNK):
            pb = k.banks[co % 2]
            for ci in range(NCHUNK):
                k.mm(pb[:, 0:TB], won[:, ci * D + co * 128:ci * D + co * 128 + 128],
                     aT[:, ci * S + t0:ci * S + t0 + TB], start=(ci == 0),
                     stop=(ci == NCHUNK - 1), name=f"wo1{co}")
            dst = mk.rT_ap(co, t0, t0 + TB)
            k.tt(dst, dst, pb[:, 0:TB], ALU.add, name=f"ra1{co}")


MODULE_EMITTERS["L1_attn"] = emit_L1_attn


def emit_L1_ffn(mk):
    k, nc = mk.k, mk.nc
    NL, INNER, DI = 6, 1024, 2730
    wqkv_d = mk.dram_in("L1_ffn_Wqkv", [NL, D, 3 * INNER])
    qs_d = mk.dram_in("L1_ffn_q_scale", [NL, 64])
    ks_d = mk.dram_in("L1_ffn_k_scale", [NL, 64])
    wo_d = mk.dram_in("L1_ffn_Wo", [NL, INNER, D])
    lng_d = mk.dram_in("L1_ffn_ln_g", [NL, D])
    lnb_d = mk.dram_in("L1_ffn_ln_b", [NL, D])
    w1_d = mk.dram_in("L1_ffn_W1", [NL, D, 2 * DI])
    b1_d = mk.dram_in("L1_ffn_b1", [NL, 2 * DI])
    w2_d = mk.dram_in("L1_ffn_W2", [NL, DI, D])
    b2_d = mk.dram_in("L1_ffn_b2", [NL, D])
    cs1_d = mk.dram_in("l1f_cosscl", [128, S], BF16)
    sn1_d = mk.dram_in("l1f_sinscl", [128, S], BF16)
    cs2_d = mk.dram_in("l1f_cosinv", [128, S], BF16)
    sn2_d = mk.dram_in("l1f_sininv", [128, S], BF16)
    pr_d = mk.dram_in("pr64", [128, 128], BF16)
    wm_d = mk.dram_in("l1f_winmask", [128, 3 * 384], BF16)

    alloc_ln_scratch(mk)
    SG = S + 1
    ix = k.alloc("f1_ix", 128, NCHUNK * SG, BF16)
    cs1 = k.alloc("f1_cs1", 128, S, BF16)
    sn1 = k.alloc("f1_sn1", 128, S, BF16)
    cs2 = k.alloc("f1_cs2", 128, S, BF16)
    sn2 = k.alloc("f1_sn2", 128, S, BF16)
    pr64 = k.alloc("f1_pr", 128, 128, BF16)
    wmask = k.alloc("f1_wm", 128, 3 * 384, BF16)
    bd64 = k.alloc("f1_bd", 128, 128, BF16)
    for buf, dd_ in ((cs1, cs1_d), (sn1, sn1_d), (cs2, cs2_d), (sn2, sn2_d),
                     (pr64, pr_d), (wmask, wm_d)):
        k.dma("sync", buf.ap(), dd_, name="ldc_" + buf.name)
    k.dma("sync", bd64.ap(), mk.dram_in("bd64", [128, 128], BF16), name="ld_bd2")
    gcol = k.alloc("f1_gcol", 128, NL * NCHUNK, F32)
    bcol = k.alloc("f1_bcol", 128, NL * NCHUNK, F32)
    b2c = k.alloc("f1_b2c", 128, NL * NCHUNK, F32)
    b1ca = k.alloc("f1_b1ca", 128, NL * 22, F32)
    b1cg = k.alloc("f1_b1cg", 128, NL * 22, F32)
    qscol = k.alloc("f1_qsc", 128, NL, F32)
    kscol = k.alloc("f1_ksc", 128, NL, F32)
    row2 = k.alloc("f1_row2", 1, 128, F32)
    for li in range(NL):
        for row_d, dst, nch in ((lng_d[li, :], (gcol, li * NCHUNK), NCHUNK),
                                (lnb_d[li, :], (bcol, li * NCHUNK), NCHUNK),
                                (b2_d[li, :], (b2c, li * NCHUNK), NCHUNK)):
            mr2 = k.mark()
            rr = load_row(mk, f"f1_r{li}_{dst[1]}", row_d, nch * 128)
            one = k.const_col(1.0)
            for ch in range(nch):
                pb = k.banks[7]
                k.mm(pb[:, 0:1], rr[0:1, ch * 128:(ch + 1) * 128], one[0:1, 0:1],
                     start=True, stop=True, name=f"cc{li}_{ch}")
                k.copy(dst[0][:, dst[1] + ch:dst[1] + ch + 1], pb[:, 0:1],
                       eng="vector", name=f"cce{li}_{ch}")
            k.release(mr2)
        mr2 = k.mark()
        rr = load_row(mk, f"f1_b1r{li}", b1_d[li, :], 2 * DI)
        one = k.const_col(1.0)
        for ch in range(22):
            m_ = min(128, DI - ch * 128)
            for part, dst in ((0, b1ca), (DI, b1cg)):
                pb = k.banks[7]
                k.mm(pb[:m_, 0:1], rr[0:1, part + ch * 128:part + ch * 128 + m_],
                     one[0:1, 0:1], start=True, stop=True, name=f"b1c{li}_{ch}")
                k.copy(dst[:m_, li * 22 + ch:li * 22 + ch + 1], pb[:m_, 0:1],
                       eng="vector", name=f"b1ce{li}_{ch}")
        k.release(mr2)
        mr2 = k.mark()
        for src_d, dst in ((qs_d[li, :], qscol), (ks_d[li, :], kscol)):
            rr = load_row(mk, f"f1_qs{li}_{dst.name}", src_d, 64)
            k.copy(row2[0:1, 0:64], rr.ap(), eng="vector", name=f"q2a{li}")
            k.copy(row2[0:1, 64:128], rr.ap(), eng="vector", name=f"q2b{li}")
            pb = k.banks[7]
            k.mm(pb[:, 0:1], row2.ap(), one[0:1, 0:1], start=True, stop=True,
                 name=f"qsb{li}")
            k.copy(dst[:, li:li + 1], pb[:, 0:1], eng="vector", name=f"qsbe{li}")
        k.release(mr2)

    def ixc(c, a, b_):
        return ix[:, c * SG + 1 + a:c * SG + 1 + b_]

    def sxc(c, a, b_):
        if c < 4:
            return ixc(c, a, b_)
        return ix[:, c * SG + a:c * SG + b_]
    for c in range(NCHUNK):
        k.memset(ix[:, c * SG:c * SG + 1], 0.0, name=f"ixg{c}")
    for tb in range(NTB):
        ln_tb(mk, tb, ix, dst_cols_per_chunk=SG, dst_t0=1 + tb * TB)

    mA = k.mark()
    sxb = k.alloc("f1_sx", 128, NCHUNK * TB, BF16)
    sq = k.alloc("f1_sq", 128, TB, BF16)
    rstd = k.alloc("f1_rstd", 128, TB, F32)
    tmpb = k.alloc("f1_tmp", 128, TB, BF16)
    m1b = k.alloc("f1_m1", 128, TB, BF16)
    ebuf = k.alloc("f1_e", 128, 384, BF16)
    nmax = k.alloc("f1_nm", 128, 1, F32)
    esum = k.alloc("f1_es", 128, 1, F32)
    vh = k.alloc("f1_vh", 128, 3 * 64, BF16)
    aTt = sxb
    mPH = k.mark()

    for li in range(NL):
        k.release(mPH)
        qf = k.alloc("f1_qf", 128, NCHUNK * TB, BF16)
        kf = k.alloc("f1_kf", 128, NCHUNK * 768, BF16)
        vf = k.alloc("f1_vf", 128, NCHUNK * 768, BF16)
        wq4 = k.alloc("f1_wq4", 128, NCHUNK * INNER, BF16)
        woN = k.alloc("f1_wo", 128, NCHUNK * D, BF16)
        k.dma("gpsimd", woN.ap().rearrange("p (c n) -> p c n", n=D),
              wo_d[li].rearrange("(c p) n -> p c n", p=128), name=f"woL{li}")
        for c in range(NCHUNK):
            k.memset(kf[:, c * 768:c * 768 + 256], 0.0, name=f"kg{li}_{c}")
            k.memset(vf[:, c * 768:c * 768 + 256], 0.0, name=f"vg{li}_{c}")
        for tb in range(NTB):
            t0 = tb * TB
            if tb > 0:
                for c in range(NCHUNK):
                    k.copy(kf[:, c * 768:c * 768 + 256],
                           kf[:, c * 768 + 512:c * 768 + 768], eng="vector",
                           name=f"kcar{li}_{c}")
                    k.copy(vf[:, c * 768:c * 768 + 256],
                           vf[:, c * 768 + 512:c * 768 + 768], eng="vector",
                           name=f"vcar{li}_{c}")
            for c in range(NCHUNK):
                k.copy(sxb[:, c * TB:(c + 1) * TB], sxc(c, t0, t0 + TB),
                       eng="vector", name=f"sx{li}_{c}")
            for third, dstb in ((0, qf), (1, kf), (2, vf)):
                k.dma("gpsimd", wq4.ap().rearrange("p (c n) -> p c n", n=INNER),
                      wqkv_d[li].rearrange("(c p) n -> p c n", p=128)
                      [:, :, third * INNER:(third + 1) * INNER], name=f"wq{li}_{third}")
                for co in range(NCHUNK):
                    pb = k.banks[co % 2]
                    for ci in range(NCHUNK):
                        k.mm(pb[:, 0:TB],
                             wq4[:, ci * INNER + co * 128:ci * INNER + co * 128 + 128],
                             sxb[:, ci * TB:(ci + 1) * TB], start=(ci == 0),
                             stop=(ci == NCHUNK - 1), name=f"qkv{li}_{third}_{co}")
                    if third == 0:
                        dst = qf[:, co * TB:(co + 1) * TB]
                    else:
                        dst = (kf if third == 1 else vf)[:, co * 768 + 256:co * 768 + 768]
                    k.copy(dst, pb[:, 0:TB], eng="vector", name=f"qkve{li}_{third}_{co}")
            for isq, buf, scol, ctab, stab in ((1, qf, qscol, cs1, sn1),
                                               (0, kf, kscol, cs2, sn2)):
                for c in range(NCHUNK):
                    src = (buf[:, c * TB:(c + 1) * TB] if isq
                           else buf[:, c * 768 + 256:c * 768 + 768])
                    k.tt(sq.ap(), src, src, ALU.mult, name=f"l2s{li}_{c}")
                    pbn = k.banks[4]
                    k.mm(pbn[:, 0:TB], bd64.ap(), sq.ap(), start=True, stop=True,
                         name=f"l2n{li}_{c}")
                    k.act(rstd.ap(), pbn[:, 0:TB], AF.Sqrt, name=f"l2sq{li}_{c}")
                    k.ts(rstd.ap(), rstd.ap(), 1e-12, None, ALU.max, name=f"l2m{li}_{c}")
                    k.recip(rstd.ap(), rstd.ap(), name=f"l2r{li}_{c}")
                    k.tt(tmpb.ap(), src, rstd.ap(), ALU.mult, name=f"l2a{li}_{c}")
                    k.ts(tmpb.ap(), tmpb.ap(), scol[:, li:li + 1], None, ALU.mult,
                         name=f"l2qs{li}_{c}")
                    pbr = k.banks[5]
                    k.mm(pbr[:, 0:TB], pr64.ap(), tmpb.ap(), start=True, stop=True,
                         name=f"rot{li}_{c}")
                    k.tt(m1b.ap(), tmpb.ap(), ctab[:, t0:t0 + TB], ALU.mult,
                         name=f"rp1{li}_{c}")
                    k.tt(tmpb.ap(), pbr[:, 0:TB], stab[:, t0:t0 + TB], ALU.mult,
                         name=f"rp2{li}_{c}")
                    k.tt(src, m1b.ap(), tmpb.ap(), ALU.add, name=f"rp3{li}_{c}")
            for t_ in range(4):
                tg = tb * 4 + t_
                mv = 0 if tg == 0 else (1 if tg == 1 else 2)
                for h in range(16):
                    ch, r0 = h // 2, 64 * (h % 2)
                    pss = k.banks[6]
                    k.mm(pss[:, 0:384],
                         qf[r0:r0 + 64, ch * TB + t_ * 128:ch * TB + (t_ + 1) * 128],
                         kf[r0:r0 + 64, ch * 768 + t_ * 128:ch * 768 + t_ * 128 + 384],
                         start=True, stop=True, name=f"qk{li}_{tg}_{h}")
                    k.tt(pss[:, 0:384], pss[:, 0:384],
                         wmask[:, mv * 384:(mv + 1) * 384], ALU.add,
                         name=f"wm{li}_{tg}_{h}")
                    k.red(nmax.ap(), pss[:, 0:384], ALU.max, negate=True,
                          name=f"mx{li}_{tg}_{h}")
                    k.ts(nmax.ap(), nmax.ap(), 8.0, None, ALU.mult, name=f"mx8{li}_{tg}_{h}")
                    k.act(ebuf.ap(), pss[:, 0:384], AF.Exp, bias=nmax[:, 0:1],
                          scale=8.0, accum_out=esum.ap(), name=f"ex{li}_{tg}_{h}")
                    k.recip(esum.ap(), esum.ap(), name=f"er{li}_{tg}_{h}")
                    k.ts(ebuf.ap(), ebuf.ap(), esum[:, 0:1], None, ALU.mult,
                         name=f"en{li}_{tg}_{h}")
                    for w_ in range(3):
                        pte = k.banks_bf[7]
                        k.trans(pte[0:128, 0:128], ebuf[:, w_ * 128:(w_ + 1) * 128],
                                mk.idb.ap(), name=f"et{li}_{tg}_{h}_{w_}")
                        k.copy(tmpb[:, w_ * 128:(w_ + 1) * 128], pte[0:128, 0:128],
                               eng="vector", name=f"ete{li}_{tg}_{h}_{w_}")
                        ptv = k.banks_bf[3]
                        k.trans(ptv[0:128, 0:64],
                                vf[r0:r0 + 64, ch * 768 + t_ * 128 + w_ * 128:
                                   ch * 768 + t_ * 128 + (w_ + 1) * 128],
                                mk.idb.ap()[r0:r0 + 64, r0:r0 + 64],
                                name=f"vt{li}_{tg}_{h}_{w_}")
                        k.copy(vh[:, w_ * 64:(w_ + 1) * 64], ptv[0:128, 0:64],
                               eng="vector", name=f"vte{li}_{tg}_{h}_{w_}")
                    pvo = k.banks[4]
                    for w_ in range(3):
                        k.mm(pvo[0:64, 0:128], vh[:, w_ * 64:(w_ + 1) * 64],
                             tmpb[:, w_ * 128:(w_ + 1) * 128], start=(w_ == 0),
                             stop=(w_ == 2), name=f"pv{li}_{tg}_{h}_{w_}")
                    k.copy(aTt[r0:r0 + 64, ch * TB + t_ * 128:ch * TB + (t_ + 1) * 128],
                           pvo[0:64, 0:128], eng="vector", name=f"pve{li}_{tg}_{h}")
            for co in range(NCHUNK):
                pb = k.banks[co % 2]
                for ci in range(NCHUNK):
                    k.mm(pb[:, 0:TB], woN[:, ci * D + co * 128:ci * D + co * 128 + 128],
                         aTt[:, ci * TB:(ci + 1) * TB], start=(ci == 0),
                         stop=(ci == NCHUNK - 1), name=f"wo{li}_{co}")
                dst = ixc(co, t0, t0 + TB)
                k.tt(dst, dst, pb[:, 0:TB], ALU.add, name=f"ixa{li}_{co}")
        k.release(mPH)
        xn2 = k.alloc("f1_xn2", 128, NCHUNK * TB, BF16)
        w1blk = k.alloc("f1_w1b", 128, NCHUNK * 256, BF16)
        gl = k.alloc("f1_gl", 128, TB, BF16)
        sfm = k.alloc("f1_sfm", 128, 22 * TB, BF16)
        w2blk = k.alloc("f1_w2b", 128, 22 * 128, BF16)
        for tb in range(NTB):
            t0 = tb * TB
            for c in range(NCHUNK):
                k.copy(sxb[:, c * TB:(c + 1) * TB], sxc(c, t0, t0 + TB),
                       eng="vector", name=f"fsx{li}_{c}")

            def srcf(c, a, b_, tb=tb):
                return sxb[:, c * TB + (a - tb * TB):c * TB + (b_ - tb * TB)]
            ln_tb(mk, tb, xn2, gamma_col=gcol[:, li * NCHUNK:(li + 1) * NCHUNK],
                  beta_col=bcol[:, li * NCHUNK:(li + 1) * NCHUNK],
                  src_fn=srcf, src_bf16=True)
            for ch in range(22):
                m_ = min(128, DI - ch * 128)
                k.dma("gpsimd", w1blk.ap().rearrange("p (c n) -> p c n", n=256)
                      [:, :, 0:m_],
                      w1_d[li].rearrange("(c p) n -> p c n", p=128)
                      [:, :, ch * 128:ch * 128 + m_], name=f"w1a{li}_{ch}")
                k.dma("gpsimd", w1blk.ap().rearrange("p (c n) -> p c n", n=256)
                      [:, :, 128:128 + m_],
                      w1_d[li].rearrange("(c p) n -> p c n", p=128)
                      [:, :, DI + ch * 128:DI + ch * 128 + m_], name=f"w1g{li}_{ch}")
                pa = k.banks[2]
                pg = k.banks[3]
                for ci in range(NCHUNK):
                    k.mm(pa[:m_, 0:TB], w1blk[:, ci * 256:ci * 256 + m_],
                         xn2[:, ci * TB:(ci + 1) * TB], start=(ci == 0),
                         stop=(ci == NCHUNK - 1), name=f"w1am{li}_{ch}")
                    k.mm(pg[:m_, 0:TB], w1blk[:, ci * 256 + 128:ci * 256 + 128 + m_],
                         xn2[:, ci * TB:(ci + 1) * TB], start=(ci == 0),
                         stop=(ci == NCHUNK - 1), name=f"w1gm{li}_{ch}")
                k.act(gl[:m_, 0:TB], pg[:m_, 0:TB], AF.Gelu,
                      bias=b1cg[:, li * 22 + ch:li * 22 + ch + 1], name=f"gel{li}_{ch}")
                k.stt(sfm[:m_, ch * TB:(ch + 1) * TB], pa[:m_, 0:TB],
                      b1ca[:, li * 22 + ch:li * 22 + ch + 1], gl[:m_, 0:TB],
                      ALU.add, ALU.mult, name=f"sfm{li}_{ch}")
            for co in range(NCHUNK):
                k.dma("gpsimd", w2blk.ap().rearrange("p (c n) -> p c n", n=128)
                      [:, 0:21, :],
                      w2_d[li, 0:2688, co * 128:(co + 1) * 128]
                      .rearrange("(c p) n -> p c n", p=128), name=f"w2L{li}_{co}")
                k.dma("gpsimd", w2blk[0:42, 21 * 128:22 * 128],
                      w2_d[li, 2688:2730, co * 128:(co + 1) * 128], name=f"w2Lp{li}_{co}")
                pb = k.banks[co % 2]
                for ch in range(22):
                    m_ = min(128, DI - ch * 128)
                    k.mm(pb[:, 0:TB], w2blk[0:m_, ch * 128:(ch + 1) * 128],
                         sfm[0:m_, ch * TB:(ch + 1) * TB], start=(ch == 0),
                         stop=(ch == 21), name=f"w2m{li}_{co}")
                dst = ixc(co, t0, t0 + TB)
                k.stt(dst, pb[:, 0:TB], b2c[:, li * NCHUNK + co:li * NCHUNK + co + 1],
                      dst, ALU.add, ALU.add, name=f"fixa{li}_{co}")
    k.release(mA)
    for tb in range(NTB):
        t0 = tb * TB
        for c in range(NCHUNK):
            dst = mk.rT_ap(c, t0, t0 + TB)
            k.tt(dst, dst, ixc(c, t0, t0 + TB), ALU.add, name=f"fin{c}")


MODULE_EMITTERS["L1_ffn"] = emit_L1_ffn


def emit_L3_attn(mk):
    k, nc = mk.k, mk.nc
    DI2, CH, NST = 2048, 2176, 64
    NCC = CH // 128
    w1_d = mk.dram_in("L3_attn_W_in1", [4256, D])
    alog_d = mk.dram_in("L3_attn_A_log", [32])
    cw_d = mk.dram_in("L3_attn_conv_w", [CH, CH, 4])
    cb_d = mk.dram_in("L3_attn_conv_b", [CH])
    pa_d = mk.dram_in("L3_attn_prelu_a", [1])
    w2_d = mk.dram_in("L3_attn_W_in2", [DI2, D])
    w3_d = mk.dram_in("L3_attn_W_in3", [D, DI2])
    lg_d = mk.dram_in("L3_attn_ln_g", [DI2])
    lb_d = mk.dram_in("L3_attn_ln_b", [DI2])
    trilf_d = mk.dram_in("trilf", [128, 128])
    rs32_d = mk.dram_in("rowsel32f", [32, 4096])
    smask_d = mk.dram_in("l3_smask", [128, 128], BF16)

    scr1 = nc.dram_tensor("scr_l3i", [2208, D], BF16, kind="Internal").ap()
    k.dma("gpsimd", scr1, w1_d[2048:4256, :], name="c_l3i")
    scr2 = wT_scratch(mk, "l3w2", w2_d, DI2, D)
    scr3 = wT_scratch(mk, "l3w3", w3_d, D, DI2)
    cwscr = nc.dram_tensor("scr_l3cw", [NCC, 128, 68 * 128], BF16, kind="Internal").ap()

    alloc_ln_scratch(mk)
    trilf = k.alloc("l3_tril", 128, 128, F32)
    rs32 = k.alloc("l3_rs32", 32, 4096, F32)
    smask = k.alloc("l3_smask", 128, 128, BF16)
    for buf, dd_ in ((trilf, trilf_d), (rs32, rs32_d), (smask, smask_d)):
        k.dma("sync", buf.ap(), dd_, name="lc_" + buf.name)
    cbc = k.alloc("l3_cbc", 128, NCC, F32)
    lgc = k.alloc("l3_lgc", 128, 16, F32)
    lbc = k.alloc("l3_lbc", 128, 16, F32)
    acol = k.alloc("l3_acol", 32, 1, F32)
    pacol = k.alloc("l3_pac", 128, 1, F32)
    mr = k.mark()
    for row_d, dst, nch in ((cb_d, cbc, NCC), (lg_d, lgc, 16), (lb_d, lbc, 16)):
        mr2 = k.mark()
        rr = load_row(mk, "l3r_" + dst.name, row_d, nch * 128)
        col_chunks_from_row(mk, rr.ap(), dst, nch)
        k.release(mr2)
    rr = load_row(mk, "l3_alog", alog_d, 32)
    k.act(rr.ap(), rr.ap(), AF.Exp, name="aexp")
    k.ts(rr.ap(), rr.ap(), -1.0, None, ALU.mult, name="aneg")
    pb = k.banks[7]
    k.mm(pb[:32, 0:1], rr[0:1, 0:32], k.const_col(1.0)[0:1, 0:1], start=True,
         stop=True, name="acolm")
    k.copy(acol.ap(), pb[:32, 0:1], eng="vector", name="acole")
    rr2 = load_row(mk, "l3_pa", pa_d, 1)
    onesr3 = k.alloc("l3_or", 1, 128, F32)
    k.memset(onesr3.ap(), 1.0)
    k.mm(pb[:, 0:1], onesr3.ap(), rr2[0:1, 0:1], start=True, stop=True, name="pam")
    k.copy(pacol.ap(), pb[:, 0:1], eng="vector", name="pae")
    k.release(mr)
    _l3sub = os.environ.get("MK_L3SUB", "")
    mq = k.mark()
    wnat = k.alloc("l3_wnat", 128, CH * 4, BF16)
    wstg = k.alloc("l3_wstg", 128, 68 * 128, BF16)
    for co in range(NCC if _l3sub != "ip" else 0):
        k.dma("gpsimd", wnat.ap(), cw_d[co * 128:(co + 1) * 128, :, :], name=f"cwl{co}")
        for ci in range(NCC):
            for kk_ in range(4):
                ptw = k.banks_bf[(ci * 4 + kk_) % 2]
                src = wnat.ap().rearrange("p (i t) -> p i t", t=4)[:, ci * 128:(ci + 1) * 128, kk_]
                k.trans(ptw[0:128, 0:128], src, mk.idb.ap(), name=f"cwt{co}_{ci}_{kk_}")
                k.copy(wstg[:, (ci * 4 + kk_) * 128:(ci * 4 + kk_ + 1) * 128],
                       ptw[0:128, 0:128], eng="vector", name=f"cwe{co}_{ci}_{kk_}")
        k.dma("gpsimd", cwscr[co], wstg.ap(), name=f"cws{co}")
    k.release(mq)

    Sf = k.alloc("l3_Sf", 64, 32 * 64, F32)
    Sb = k.alloc("l3_Sb", 64, 32 * 64, BF16)
    k.memset(Sf.ap(), 0.0, name="s0")
    k.memset(Sb.ap(), 0.0, name="s0b")
    ub = k.alloc("l3_ub", 128, NCHUNK * TB, BF16)
    xconv = k.alloc("l3_xc", 128, NCC * TB, BF16)
    dtf = k.alloc("l3_dtf", 32, TB, F32)
    atb = k.alloc("l3_atb", 32, TB, F32)
    tails = k.alloc("l3_tails", 128, NCC * 3, BF16)
    spt = k.alloc("l3_spt", 32, TB, F32)
    mPH = k.mark()

    GRP = [(0, 512), (512, 512), (1024, 512), (1536, 512), (2048, 160)]
    for tb in range(NTB):
        t0 = tb * TB
        k.release(mPH)
        xbc = k.alloc("l3_xbc", 128, NCC * 515, BF16)
        wbuf = k.alloc("l3_wb", 128, NCHUNK * 512, BF16)
        wcv = k.alloc("l3_wcv", 128, 68 * 128, BF16)
        ln_tb(mk, tb, ub)
        for ci in range(NCC):
            if tb == 0:
                k.memset(xbc[:, ci * 515:ci * 515 + 3], 0.0, name=f"xg{ci}")
            else:
                k.copy(xbc[:, ci * 515:ci * 515 + 3], tails[:, ci * 3:(ci + 1) * 3],
                       eng="vector", name=f"xgc{ci}")
        for gi, (g0, gn) in enumerate(GRP):
            for ci in range(NCHUNK):
                k.dma("sync", wbuf[:, ci * 512:ci * 512 + gn],
                      scr1[g0:g0 + gn, ci * 128:(ci + 1) * 128], transpose=True,
                      name=f"ipw{gi}_{ci}")
            nco = (gn + 127) // 128
            for col in range(nco):
                m_ = min(128, gn - col * 128)
                co = (g0 + col * 128) // 128
                pbp = k.banks[col % 2]
                for ci in range(NCHUNK):
                    k.mm(pbp[:m_, 0:TB], wbuf[:, ci * 512 + col * 128:ci * 512 + col * 128 + m_],
                         ub[:, ci * TB:(ci + 1) * TB], start=(ci == 0),
                         stop=(ci == NCHUNK - 1), name=f"ip{gi}_{col}")
                if co < NCC:
                    k.copy(xbc[:, co * 515 + 3:co * 515 + 515], pbp[:, 0:TB],
                           eng="vector", name=f"ipe{co}")
                else:
                    k.copy(dtf.ap(), pbp[:32, 0:TB], eng="vector", name="ipdt")
        for ci in range(NCC):
            k.copy(tails[:, ci * 3:(ci + 1) * 3], xbc[:, ci * 515 + 512:ci * 515 + 515],
                   eng="vector", name=f"tl{ci}")
        k.act(spt.ap(), dtf.ap(), AF.Abs, name="spabs")
        k.act(spt.ap(), spt.ap(), AF.Exp, scale=-1.0, name="spexp")
        k.act(spt.ap(), spt.ap(), AF.Ln, bias=k.const_col(1.0)[0:32, 0:1], name="spln")
        k.ts(atb.ap(), dtf.ap(), 0.0, None, ALU.max, name="sprelu")
        k.tt(spt.ap(), spt.ap(), atb.ap(), ALU.add, name="spadd")
        k.ts(atb.ap(), spt.ap(), acol[0:32, 0:1], None, ALU.mult, name="atmul")
        for co in range(NCC if _l3sub not in ("ip", "cw") else 0):
            k.dma("gpsimd", wcv.ap(), cwscr[co], name=f"cvw{co}")
            pbc = k.banks[2 + co % 2]
            first = True
            _nci = int(os.environ.get("MK_L3CI", str(NCC)))
            for ci in range(_nci):
                for kk_ in range(4):
                    k.mm(pbc[:, 0:TB], wcv[:, (ci * 4 + kk_) * 128:(ci * 4 + kk_ + 1) * 128],
                         xbc[:, ci * 515 + kk_:ci * 515 + kk_ + TB],
                         start=first, stop=(ci == _nci - 1 and kk_ == 3),
                         name=f"cv{co}_{ci}_{kk_}")
                    first = False
            if os.environ.get("MK_L3NP"):
                k.copy(xconv[:, co * TB:(co + 1) * TB], pbc[:, 0:TB], eng="vector",
                       name=f"cr{co}")
            else:
                k.act(xconv[:, co * TB:(co + 1) * TB], pbc[:, 0:TB], AF.Relu,
                      bias=cbc[:, co:co + 1], name=f"cr{co}")
                k.ts(ub[:, 0:TB], pbc[:, 0:TB], cbc[:, co:co + 1], None, ALU.add,
                     name=f"cm{co}")
                k.ts(ub[:, 0:TB], ub[:, 0:TB], 0.0, None, ALU.min, name=f"cm2{co}")
                k.ts(ub[:, 0:TB], ub[:, 0:TB], pacol[:, 0:1], None, ALU.mult,
                     name=f"cm3{co}")
                k.tt(xconv[:, co * TB:(co + 1) * TB], xconv[:, co * TB:(co + 1) * TB],
                     ub[:, 0:TB], ALU.add, name=f"cp{co}")
        if os.environ.get("MK_L3PHASE", "3") == "1":
            continue
        k.release(mPH)
        xtok = k.alloc("l3_xtok", 128, DI2, BF16)
        ytok = k.alloc("l3_ytok", 128, DI2, BF16)
        btok = k.alloc("l3_btok", 128, 64, BF16)
        at_s = k.alloc("l3_ats", 128, 32, F32)
        csb = k.alloc("l3_cs", 128, 32, F32)
        csT = k.alloc("l3_csT", 32, 128, F32)
        gsb = k.alloc("l3_gsb", 128, 128, BF16)
        lsb = k.alloc("l3_lsb", 128, 128, BF16)
        segb = k.alloc("l3_seg", 128, 128, F32)
        y1sb = k.alloc("l3_y1", 128, 64, F32)
        sodc = k.alloc("l3_sod", 128, 1, F32)
        dsdc = k.alloc("l3_dsd", 128, 1, F32)
        dchb = k.alloc("l3_dchb", 128, 1, F32)
        xds = k.alloc("l3_xds", 128, 64, BF16)
        cfm = k.alloc("l3_cfm", 64, 128, BF16)
        for c4 in range(4):
            cc0 = c4 * 128
            pta = k.banks[4]
            k.trans(pta[0:128, 0:32], atb[:, cc0:cc0 + 128], mk.idf.ap()[0:32, 0:32],
                    name=f"att{c4}")
            k.copy(at_s.ap(), pta[0:128, 0:32], eng="vector", name=f"atte{c4}")
            pcs = k.banks[5]
            k.mm(pcs[:, 0:32], trilf.ap(), at_s.ap(), start=True, stop=True,
                 name=f"cs{c4}")
            k.copy(csb.ap(), pcs[:, 0:32], eng="vector", name=f"cse{c4}")
            pct = k.banks[4]
            k.trans(pct[0:32, 0:128], csb.ap(), mk.idf.ap(), name=f"csT{c4}")
            k.copy(csT.ap(), pct[0:32, 0:128], eng="vector", name=f"csTe{c4}")
            for ch in range(16):
                ptx = k.banks_bf[6 + ch % 2]
                k.trans(ptx[0:128, 0:128], xconv[:, ch * TB + cc0:ch * TB + cc0 + 128],
                        mk.idb.ap(), name=f"xt{c4}_{ch}")
                k.copy(xtok[:, ch * 128:(ch + 1) * 128], ptx[0:128, 0:128],
                       eng="vector", name=f"xte{c4}_{ch}")
            ptb = k.banks_bf[6]
            k.trans(ptb[0:128, 0:64], xconv[0:64, 16 * TB + cc0:16 * TB + cc0 + 128],
                    mk.idb.ap()[0:64, 0:64], name=f"bt{c4}")
            k.copy(btok.ap(), ptb[0:128, 0:64], eng="vector", name=f"bte{c4}")
            k.copy(cfm.ap(), xconv[64:128, 16 * TB + cc0:16 * TB + cc0 + 128],
                   eng="vector", name=f"cfm{c4}")
            pg = k.banks[5]
            k.mm(pg[:, 0:128], xconv[0:64, 16 * TB + cc0:16 * TB + cc0 + 128],
                 cfm.ap(), start=True, stop=True, name=f"g{c4}")
            k.copy(gsb.ap(), pg[:, 0:128], eng="vector", name=f"ge{c4}")
            for h in range(32):
                pbc2 = k.banks[5]
                k.mm(pbc2[:, 0:128], rs32[:, h * 128:(h + 1) * 128], csT.ap(),
                     start=True, stop=True, name=f"csbc{c4}_{h}")
                k.ts(segb.ap(), pbc2[:, 0:128], csb[:, h:h + 1], None, ALU.subtract,
                     name=f"seg{c4}_{h}")
                k.tt(segb.ap(), segb.ap(), smask.ap(), ALU.add, name=f"segm{c4}_{h}")
                k.act(lsb.ap(), segb.ap(), AF.Exp, name=f"lexp{c4}_{h}")
                k.tt(lsb.ap(), lsb.ap(), gsb.ap(), ALU.mult, name=f"mt{c4}_{h}")
                k.act(sodc.ap(), csb[:, h:h + 1], AF.Exp, name=f"sod{c4}_{h}")
                pdc = k.banks[4]
                k.mm(pdc[:, 0:1], rs32[:, h * 128:(h + 1) * 128], csT[:, 127:128],
                     start=True, stop=True, name=f"dch{c4}_{h}")
                k.copy(dchb.ap(), pdc[:, 0:1], eng="vector", name=f"dchbe{c4}_{h}")
                k.act(dsdc.ap(), csb[:, h:h + 1], AF.Exp, scale=-1.0,
                      bias=dchb[:, 0:1], name=f"dsd{c4}_{h}")
                py1 = k.banks[2]
                k.mm(py1[:, 0:64], lsb.ap(), xtok[:, h * 64:(h + 1) * 64],
                     start=True, stop=True, name=f"yd{c4}_{h}")
                py2 = k.banks[3]
                k.mm(py2[:, 0:64], cfm.ap(), Sb[:, h * 64:(h + 1) * 64],
                     start=True, stop=True, name=f"yo{c4}_{h}")
                k.copy(y1sb.ap(), py1[:, 0:64], eng="vector", name=f"y1e{c4}_{h}")
                k.stt(ytok[:, h * 64:(h + 1) * 64], py2[:, 0:64], sodc[:, 0:1],
                      y1sb.ap(), ALU.mult, ALU.add, name=f"yc{c4}_{h}")
                k.ts(xds.ap(), xtok[:, h * 64:(h + 1) * 64], dsdc[:, 0:1], None,
                     ALU.mult, name=f"xd{c4}_{h}")
                pst = k.banks[2]
                k.mm(pst[:64, 0:64], btok.ap(), xds.ap(), start=True, stop=True,
                     name=f"st{c4}_{h}")
                pdc2 = k.banks[4]
                k.mm(pdc2[:64, 0:1], rs32[:, h * 128:h * 128 + 64], csT[:, 127:128],
                     start=True, stop=True, name=f"dch2{c4}_{h}")
                k.copy(dchb[0:64, 0:1], pdc2[:64, 0:1], eng="vector", name=f"dch2e{c4}_{h}")
                k.act(dchb[0:64, 0:1], dchb[0:64, 0:1], AF.Exp, name=f"dchx{c4}_{h}")
                sl = Sf[:, h * 64:(h + 1) * 64]
                k.ts(sl, sl, dchb[0:64, 0:1], None, ALU.mult, name=f"sdec{c4}_{h}")
                k.tt(sl, sl, pst[:64, 0:64], ALU.add, name=f"sacc{c4}_{h}")
                k.copy(Sb[:, h * 64:(h + 1) * 64], sl, eng="vector", name=f"sbe{c4}_{h}")
            for ch in range(16):
                pty = k.banks_bf[6 + ch % 2]
                k.trans(pty[0:128, 0:128], ytok[:, ch * 128:(ch + 1) * 128],
                        mk.idb.ap(), name=f"yt{c4}_{ch}")
                k.copy(xconv[:, ch * TB + cc0:ch * TB + cc0 + 128], pty[0:128, 0:128],
                       eng="vector", name=f"yte{c4}_{ch}")
        if os.environ.get("MK_L3PHASE", "3") == "2":
            continue
        k.release(mPH)
        w23 = k.alloc("l3_w23", 128, NCHUNK * DI2, BF16)
        u2T = k.alloc("l3_u2T", 128, 16 * TB, BF16)
        mu2 = k.alloc("l3_mu2", 128, TB, F32)
        rstd3 = k.alloc("l3_rstd3", 128, TB, F32)
        tmp3 = k.alloc("l3_tmp3", 128, TB, F32)
        sqb = k.alloc("l3_sqb", 128, TB, BF16)
        ln_tb(mk, tb, ub)
        for ci in range(NCHUNK):
            k.dma("sync", w23[:, ci * DI2:(ci + 1) * DI2],
                  scr2[:, ci * 128:(ci + 1) * 128], transpose=True, name=f"w2T{ci}")
        for co in range(16):
            pbu = k.banks[co % 2]
            for ci in range(NCHUNK):
                k.mm(pbu[:, 0:TB], w23[:, ci * DI2 + co * 128:ci * DI2 + co * 128 + 128],
                     ub[:, ci * TB:(ci + 1) * TB], start=(ci == 0),
                     stop=(ci == NCHUNK - 1), name=f"u2{co}")
            k.act(u2T[:, co * TB:(co + 1) * TB], pbu[:, 0:TB], AF.Gelu, name=f"u2g{co}")
            k.tt(u2T[:, co * TB:(co + 1) * TB], u2T[:, co * TB:(co + 1) * TB],
                 xconv[:, co * TB:(co + 1) * TB], ALU.mult, name=f"hm{co}")
        ps_s = k.banks[6]
        ps_q = k.banks[7]
        for ci in range(16):
            hsl = u2T[:, ci * TB:(ci + 1) * TB]
            k.act(sqb.ap(), hsl, AF.Square, name=f"lsq{ci}")
            k.mm(ps_s[:, 0:TB], mk.oneb.ap(), hsl, start=(ci == 0), stop=(ci == 15),
                 name=f"lsum{ci}")
            k.mm(ps_q[:, 0:TB], mk.oneb.ap(), sqb.ap(), start=(ci == 0), stop=(ci == 15),
                 name=f"lsq2{ci}")
        k.ts(mu2.ap(), ps_s[:, 0:TB], 1.0 / DI2, None, ALU.mult, name="lmu")
        k.ts(tmp3.ap(), ps_q[:, 0:TB], 1.0 / DI2, 1e-5, ALU.mult, ALU.add, name="lms")
        k.act(rstd3.ap(), mu2.ap(), AF.Square, name="lmu2")
        k.tt(tmp3.ap(), tmp3.ap(), rstd3.ap(), ALU.subtract, name="lvar")
        k.act(rstd3.ap(), tmp3.ap(), AF.Sqrt, name="lsqr")
        k.recip(rstd3.ap(), rstd3.ap(), name="lrec")
        for ci in range(16):
            hsl = u2T[:, ci * TB:(ci + 1) * TB]
            k.tt(tmp3.ap(), hsl, mu2.ap(), ALU.subtract, name=f"lap{ci}")
            k.tt(tmp3.ap(), tmp3.ap(), rstd3.ap(), ALU.mult, name=f"lap2{ci}")
            k.ts(hsl, tmp3.ap(), lgc[:, ci:ci + 1], lbc[:, ci:ci + 1],
                 ALU.mult, ALU.add, name=f"lgb{ci}")
        for ci in range(16):
            k.dma("sync", w23[:, ci * 1024:(ci + 1) * 1024],
                  scr3[:, ci * 128:(ci + 1) * 128], transpose=True, name=f"w3T{ci}")
        for co in range(NCHUNK):
            pbo = k.banks[2 + co % 2]
            for ci in range(16):
                k.mm(pbo[:, 0:TB], w23[:, ci * 1024 + co * 128:ci * 1024 + co * 128 + 128],
                     u2T[:, ci * TB:(ci + 1) * TB], start=(ci == 0), stop=(ci == 15),
                     name=f"w3m{co}")
            dst = mk.rT_ap(co, t0, t0 + TB)
            k.tt(dst, dst, pbo[:, 0:TB], ALU.add, name=f"l3ra{co}")


MODULE_EMITTERS["L3_attn"] = emit_L3_attn



def _run(inputs, modules, dumps=()):
    nc, in_names = build_program(modules, dumps)
    x = np.ascontiguousarray(inputs["L0_attn_x"], dtype=np.float32)
    in_maps = []
    for b in range(B):
        m = {}
        for n in in_names:
            if n == "x":
                m[n] = x[b]
            elif n in CONSTS:
                m[n] = CONSTS[n]
            else:
                v = np.ascontiguousarray(inputs[n])
                if v.ndim == 0:
                    v = v.reshape(1)
                m[n] = v
        in_maps.append(m)
    res = run_bass_kernel_spmd(nc, in_maps, core_ids=list(range(B)))
    return res


CHAIN = ["L0_attn", "L0_ffn", "L1_attn", "L1_ffn",
         "L2_attn", "L2_ffn", "L3_attn", "L3_ffn"]


def kernel(**inputs):
    modules = list(CHAIN)
    want = os.environ.get("MK_MODULES")
    if want is not None:
        modules = [m for m in want.split(",") if m]
    res = _run(inputs, modules)
    out = np.stack([np.asarray(r["y"], dtype=np.float32) for r in res.results], axis=0)
    return out
